# version 07
# baseline (speedup 1.0000x reference)
import sys
import numpy as np

sys.path.insert(0, "/opt/trn_rl_repo")

import ml_dtypes
import concourse.bass as bass
import concourse.mybir as mybir
from concourse.bass_utils import run_bass_kernel_spmd
from contextlib import ExitStack

F32 = mybir.dt.float32
BF16 = mybir.dt.bfloat16
U8 = mybir.dt.uint8
AF = mybir.ActivationFunctionType
ALU = mybir.AluOpType
AX = mybir.AxisListType.X
bfdt = ml_dtypes.bfloat16

E = 2048
T = 4096
B = 4
NCORE = 8
RE = 2560
OWN = 2048
PADQ = 512
KEYS = 4096
LNEPS = 1e-5
HD = 64
NDH = 16
VD = 128
SCALE_A = HD ** -0.5
LAMBDA_INIT = 0.8 - 0.6 * float(np.exp(-0.3 * 12))
RMSEPS = 1e-6
HB = 16
HDB = 128
WIN = 512
SCALE_B = HDB ** -0.5
FI = 8192
NIC = 16
QB_N = 5
KMAX = [min(KEYS, qb * 512 + 2048) for qb in range(QB_N)]

S_E = [4 * m + s for m in range(8) for s in (0, 1)]
S_O = [4 * m + 2 + s for m in range(8) for s in (0, 1)]


def _qperm():
    cols = []
    for c in range(16):
        for s in (S_E[c], S_O[c]):
            cols.extend(range(s * HD, (s + 1) * HD))
    return np.array(cols)


def _npad_host(h):
    q = np.arange(RE)
    if h == 1:
        v = np.zeros(RE)
    else:
        sched = q + 1537.0
        v = np.where(q < PADQ, sched - 1.0, np.minimum(2048.0, sched))
    return np.ascontiguousarray(v.reshape(20, 128).T.astype(np.float32))


def _dmask_host():
    m = np.zeros((4, 128, 1024), dtype=bfdt)
    i = np.arange(128)[:, None]
    j = np.arange(512)[None, :]
    for o in range(4):
        mm = (i <= j - 128 * o).astype(bfdt)
        m[o, :, 0:512] = mm
        m[o, :, 512:1024] = mm
    return m


def _bmask_host(h):
    m = np.zeros((16, 5, 128, 128), dtype=bfdt)
    i = np.arange(128)[:, None]
    j = np.arange(128)[None, :]
    for qt in range(16):
        for et in range(5):
            dd = et * 128 + i - j
            v = (dd >= 0) & (dd <= WIN)
            if h == 0:
                e_abs = (qt + et) * 128 + i
                v = v & np.broadcast_to(e_abs >= PADQ, v.shape)
            m[qt, et] = v.astype(bfdt)
    return m


def _rope_host(h):
    pos = np.maximum(np.arange(KEYS) + (h - 1) * 2048, 0).astype(np.float64)
    inv = 1.0 / 10000.0 ** (np.arange(0, HD, 2) / HD)
    fr = pos[:, None] * inv[None, :]
    fr = fr.reshape(32, 128, 32).transpose(1, 0, 2)
    return (np.ascontiguousarray(np.cos(fr).astype(bfdt)),
            np.ascontiguousarray(np.sin(fr).astype(bfdt)))


def stage_inputs(inputs):
    x = np.ascontiguousarray(inputs["L0_attn_a_x"], dtype=np.float32)
    shared = {
        "wq": np.ascontiguousarray(inputs["L0_attn_a_Wq"][:, _qperm()], dtype=np.float32),
        "wk": np.ascontiguousarray(inputs["L0_attn_a_Wk"], dtype=np.float32),
        "wv": np.ascontiguousarray(inputs["L0_attn_a_Wv"], dtype=np.float32),
        "wo": np.ascontiguousarray(inputs["L0_attn_a_Wo"], dtype=np.float32),
        "bwq": np.ascontiguousarray(inputs["L0_attn_b_wq"], dtype=np.float32),
        "bwk": np.ascontiguousarray(inputs["L0_attn_b_wk"], dtype=np.float32),
        "bwv": np.ascontiguousarray(inputs["L0_attn_b_wv"], dtype=np.float32),
        "bwo": np.ascontiguousarray(inputs["L0_attn_b_wo"], dtype=np.float32),
        "wg": np.ascontiguousarray(inputs["L0_ffn_w_gate"], dtype=np.float32),
        "wu": np.ascontiguousarray(inputs["L0_ffn_w_up"], dtype=np.float32),
        "wd": np.ascontiguousarray(inputs["L0_ffn_w_down"], dtype=np.float32),
        "lq1": np.ascontiguousarray(inputs["L0_attn_a_lam_q1"], dtype=np.float32).reshape(64, 1),
        "lk1": np.ascontiguousarray(inputs["L0_attn_a_lam_k1"], dtype=np.float32).reshape(64, 1),
        "lq2": np.ascontiguousarray(inputs["L0_attn_a_lam_q2"], dtype=np.float32).reshape(64, 1),
        "lk2": np.ascontiguousarray(inputs["L0_attn_a_lam_k2"], dtype=np.float32).reshape(64, 1),
        "g": np.ascontiguousarray(inputs["L0_attn_a_g"], dtype=np.float32).reshape(1, 128),
        "identb": np.eye(128, dtype=bfdt),
        "identf": np.eye(128, dtype=np.float32),
        "dmask": _dmask_host(),
    }
    per_h = {}
    for h in (0, 1):
        rc, rs = _rope_host(h)
        per_h[h] = {"rcos": rc, "rsin": rs, "npad": _npad_host(h),
                    "bmask": _bmask_host(h)}
    maps = []
    for c in range(NCORE):
        bb, h = c // 2, c % 2
        if h == 0:
            xin = np.concatenate([np.zeros((2048, E), np.float32), x[bb, 0:2048]], axis=0)
        else:
            xin = np.ascontiguousarray(x[bb])
        m = {"xin": xin}
        m.update(shared)
        m.update(per_h[h])
        maps.append(m)
    return maps


ENGS = ("sync", "scalar", "vector", "tensor", "gpsimd")


class Bld:
    def __init__(self, nc, ctx):
        self.nc = nc
        self.ops = {e: [] for e in ENGS}
        self.sems = {}
        self.cnt = {}
        self.owner = {}
        self.waited = {e: {} for e in ENGS}
        for name, owner in (("qs", None), ("qg", None), ("spe", "tensor"),
                            ("sdv", "vector"), ("sac", "scalar"), ("spl", "gpsimd")):
            self.sems[name] = ctx.enter_context(nc.semaphore(name))
            self.cnt[name] = 0
            self.owner[name] = owner

    def op(self, eng, fn, wait=(), inc=None, n=1, swait=()):
        wl = []
        for s, v in wait:
            if v <= 0:
                continue
            assert v <= self.cnt[s], f"wait {s}>={v} but only {self.cnt[s]} emitted"
            if self.owner.get(s) == eng:
                continue
            if self.waited[eng].get(s, 0) >= v:
                continue
            self.waited[eng][s] = v
            wl.append((self.sems[s], v))
        for s, v in swait:
            if v <= 0:
                continue
            assert v <= self.cnt[s], f"swait {s}>={v} but only {self.cnt[s]} emitted"
            if self.waited[eng].get(s, 0) >= v:
                continue
            self.waited[eng][s] = v
            wl.append((self.sems[s], v))
        sem = None
        if inc is not None:
            self.cnt[inc] += n
            sem = self.sems[inc]
        self.ops[eng].append((wl, fn, sem, n))
        return self.cnt[inc] if inc else 0

    def dma(self, eng, dst, src, wait=(), **kw):
        q = "qg" if eng == "gpsimd" else "qs"
        return self.op(eng, lambda e: e.dma_start(dst, src, **kw), wait, q, 16)

    def now(self, s):
        return self.cnt[s]

    def all_now(self, *names):
        return [(s, self.cnt[s]) for s in names]

    def emit(self):
        blk_ctx = self.nc.Block()
        blk = blk_ctx.__enter__()
        try:
            for eng, meth in (("gpsimd", blk.gpsimd), ("tensor", blk.tensor),
                              ("vector", blk.vector), ("scalar", blk.scalar),
                              ("sync", blk.sync)):
                ops = self.ops[eng]

                def body(e, ops=ops):
                    for wl, fn, sem, n in ops:
                        for s, v in wl:
                            e.wait_ge(s, v)
                        ins = fn(e)
                        if sem is not None:
                            ins.then_inc(sem, n)
                meth(body)
        finally:
            blk_ctx.__exit__(None, None, None)


def _wsplit(b, eng, dst3, src3, groups, wait=()):
    G = src3.shape[0]
    step = (G + groups - 1) // groups
    r = 0
    for g0 in range(0, G, step):
        g1 = min(G, g0 + step)
        r = b.dma(eng, dst3[:, g0:g1, :], src3[g0:g1].rearrange("g p n -> p g n"),
                  wait=wait if g0 == 0 else ())
    return r


def build_program(stop_after=None, dbg=False):
    nc = bass.Bass("TRN2", target_bir_lowering=False)
    d = {}
    d["xin"] = nc.dram_tensor("xin", [KEYS, E], F32, kind="ExternalInput").ap()
    for n, sh in (("wq", [E, E]), ("wk", [E, 1024]), ("wv", [E, 1024]), ("wo", [E, E]),
                  ("bwq", [E, E]), ("bwk", [E, 512]), ("bwv", [E, 512]), ("bwo", [E, E]),
                  ("wg", [E, FI]), ("wu", [E, FI]), ("wd", [FI, E])):
        d[n] = nc.dram_tensor(n, sh, F32, kind="ExternalInput").ap()
    for n in ("lq1", "lk1", "lq2", "lk2"):
        d[n] = nc.dram_tensor(n, [64, 1], F32, kind="ExternalInput").ap()
    d["g"] = nc.dram_tensor("g", [1, 128], F32, kind="ExternalInput").ap()
    d["identb"] = nc.dram_tensor("identb", [128, 128], BF16, kind="ExternalInput").ap()
    d["identf"] = nc.dram_tensor("identf", [128, 128], F32, kind="ExternalInput").ap()
    d["dmask"] = nc.dram_tensor("dmask", [4, 128, 1024], BF16, kind="ExternalInput").ap()
    d["rcos"] = nc.dram_tensor("rcos", [128, 32, 32], BF16, kind="ExternalInput").ap()
    d["rsin"] = nc.dram_tensor("rsin", [128, 32, 32], BF16, kind="ExternalInput").ap()
    d["npad"] = nc.dram_tensor("npad", [128, 20], F32, kind="ExternalInput").ap()
    d["bmask"] = nc.dram_tensor("bmask", [16, 5, 128, 128], BF16, kind="ExternalInput").ap()
    d["out"] = nc.dram_tensor("out", [OWN, E], F32, kind="ExternalOutput").ap()
    KI = "ExternalOutput" if dbg else "Internal"
    d["xres"] = nc.dram_tensor("xres", [RE, E], F32, kind=KI).ap()
    d["xnt"] = nc.dram_tensor("xnt", [32, 128, E], BF16, kind=KI).ap()
    d["qa"] = nc.dram_tensor("qa", [20, 128, E], BF16, kind=KI).ap()
    d["ka"] = nc.dram_tensor("ka", [32, 128, 1024], BF16, kind=KI).ap()
    d["va"] = nc.dram_tensor("va", [32, 128, 1024], BF16, kind=KI).ap()
    d["anorm"] = nc.dram_tensor("anorm", [16, 128, RE], BF16, kind=KI).ap()
    d["xntb"] = nc.dram_tensor("xntb", [20, 128, E], BF16, kind=KI).ap()
    if dbg:
        d["dbg1"] = nc.dram_tensor("dbg1", [4, 128, E], BF16, kind="ExternalOutput").ap()
        d["dbg2"] = nc.dram_tensor("dbg2", [4, 128, 16], F32, kind="ExternalOutput").ap()

    with ExitStack() as ctx:
        arena = ctx.enter_context(nc.sbuf_tensor("arena", [128, 212832], U8)).ap()
        ps = ctx.enter_context(nc.psum_tensor("ps", [128, 4096], F32)).ap()

        def bank(i, j0=0, n=512):
            return ps[:, i * 512 + j0:i * 512 + j0 + n]

        b = Bld(nc, ctx)

        persist = [212832]

        def palloc(cols, dt, rows=128):
            nb = 4 if dt == F32 else (2 if dt == BF16 else 1)
            persist[0] -= cols * nb
            persist[0] -= persist[0] % 32
            return arena[0:rows, persist[0]:persist[0] + cols * nb].bitcast(dt)

        identb = palloc(128, BF16)
        identf = palloc(128, F32)
        geff = palloc(128, F32)
        nlam = palloc(1, F32)
        npadc = palloc(20, F32)
        onesb = palloc(1, BF16)
        onesf = palloc(1, F32)
        onesrow = palloc(128, F32, rows=1)
        epsb = palloc(1, F32)
        smallf = palloc(16, F32)
        lnst = palloc(16, F32)
        PBASE = persist[0]

        cur = [0]

        def alloc(cols, dt, rows=128):
            nb = 4 if dt == F32 else (2 if dt == BF16 else 1)
            cur[0] = (cur[0] + 31) // 32 * 32
            r = arena[0:rows, cur[0]:cur[0] + cols * nb].bitcast(dt)
            cur[0] += cols * nb
            return r

        def phase_reset(floor=0):
            assert cur[0] <= PBASE, f"arena overflow: {cur[0]} > {PBASE}"
            cur[0] = floor

        b.dma("gpsimd", identb, d["identb"])
        b.dma("gpsimd", identf, d["identf"])
        b.dma("gpsimd", npadc, d["npad"])
        lamv = [alloc(1, F32, rows=64) for _ in range(4)]
        for i, n in enumerate(("lq1", "lk1", "lq2", "lk2")):
            b.dma("gpsimd", lamv[i], d[n])
        grow = alloc(128, F32, rows=1)
        b.dma("gpsimd", grow, d["g"])
        b.op("gpsimd", lambda e: e.memset(onesb, 1.0), inc="spl")
        b.op("gpsimd", lambda e: e.memset(onesf, 1.0), inc="spl")
        b.op("gpsimd", lambda e: e.memset(onesrow, 1.0), inc="spl")
        b.op("gpsimd", lambda e: e.memset(epsb, LNEPS), inc="spl")
        b.op("tensor", lambda e: e.matmul(bank(0, 0, 1)[0:1, :], lamv[0], lamv[1],
                                          start=True, stop=True), wait=b.all_now("qg", "spl"))
        b.op("tensor", lambda e: e.matmul(bank(0, 1, 1)[0:1, :], lamv[2], lamv[3],
                                          start=True, stop=True), inc="spe")
        b.op("scalar", lambda e: e.activation(smallf[0:1, 0:2], bank(0, 0, 2)[0:1, :], AF.Exp),
             wait=b.all_now("spe"), inc="sac")
        h0 = b.op("vector", lambda e: e.tensor_sub(smallf[0:1, 2:3], smallf[0:1, 0:1], smallf[0:1, 1:2]),
                  wait=b.all_now("sac"), inc="sdv")
        b.op("vector", lambda e: e.tensor_scalar(smallf[0:1, 3:4], smallf[0:1, 2:3],
                                                 LAMBDA_INIT, -1.0, ALU.add, ALU.mult),
             swait=[("sdv", h0)], inc="sdv")
        b.op("tensor", lambda e: e.matmul(bank(1, 0, 1), onesrow, smallf[0:1, 3:4],
                                          start=True, stop=True), wait=b.all_now("sdv"))
        b.op("tensor", lambda e: e.matmul(bank(2, 0, 128), onesrow, grow,
                                          start=True, stop=True), inc="spe")
        b.op("vector", lambda e: e.tensor_copy(nlam, bank(1, 0, 1)), wait=b.all_now("spe"))
        b.op("vector", lambda e: e.tensor_scalar_mul(geff, bank(2, 0, 128), 1.0 - LAMBDA_INIT),
             inc="sdv")
        P0_done = b.all_now("sdv", "spl", "qg")

        def emit_ln(src, sqdump, xn_dst, wait_ld=(), war_apply=()):
            b.op("vector", lambda e: e.reduce_sum(lnst[:, 0:1], src, AX), wait=wait_ld)
            f0 = b.op("vector", lambda e: e.scalar_tensor_tensor(
                sqdump, src, 1.0, src, ALU.mult, ALU.mult, accum_out=lnst[:, 1:2]),
                wait=war_apply, inc="sdv")
            b.op("vector", lambda e: e.tensor_scalar_mul(lnst[:, 2:3], lnst[:, 0:1], 1.0 / E),
                 swait=[("sdv", f0)])
            f1 = b.op("vector", lambda e: e.tensor_scalar_mul(lnst[:, 3:4], lnst[:, 1:2], 1.0 / E),
                      inc="sdv")
            f2 = b.op("vector", lambda e: e.tensor_mul(lnst[:, 4:5], lnst[:, 2:3], lnst[:, 2:3]),
                      swait=[("sdv", f1)], inc="sdv")
            b.op("vector", lambda e: e.tensor_sub(lnst[:, 5:6], lnst[:, 3:4], lnst[:, 4:5]),
                 swait=[("sdv", f2)], inc="sdv")
            b.op("scalar", lambda e: e.activation(lnst[:, 6:7], lnst[:, 5:6], AF.Sqrt,
                                                  bias=epsb), wait=b.all_now("sdv"), inc="sac")
            r1 = b.op("vector", lambda e: e.reciprocal(lnst[:, 7:8], lnst[:, 6:7]),
                      wait=b.all_now("sac"), inc="sdv")
            r2 = b.op("vector", lambda e: e.tensor_scalar(
                lnst[:, 8:9], lnst[:, 2:3], lnst[:, 7:8], -1.0, ALU.mult, ALU.mult),
                swait=[("sdv", r1)], inc="sdv")
            return b.op("vector", lambda e: e.tensor_scalar(
                xn_dst, src, lnst[:, 7:8], lnst[:, 8:9], ALU.mult, ALU.add),
                swait=[("sdv", r2)], inc="sdv")

        def emit_rope_tm(buf, nh, cosap, sinap, tmp):
            v3 = buf.rearrange("p (c n) -> p c n", c=nh)
            x1 = v3[:, :, 0:32]
            x2 = v3[:, :, 32:64]
            cb = cosap.unsqueeze(1).broadcast_to([128, nh, 32])
            sb = sinap.unsqueeze(1).broadcast_to([128, nh, 32])
            tv = [tmp[:, i * nh * 32:(i + 1) * nh * 32].rearrange(
                "p (c n) -> p c n", c=nh) for i in range(3)]
            b.op("vector", lambda e: e.tensor_tensor(tv[0], x1, cb, ALU.mult))
            b.op("vector", lambda e: e.tensor_tensor(tv[1], x1, sb, ALU.mult))
            f0 = b.op("vector", lambda e: e.tensor_tensor(tv[2], x2, sb, ALU.mult),
                      inc="sdv")
            b.op("vector", lambda e: e.tensor_add(x1, tv[0], tv[2]), swait=[("sdv", f0)])
            f1 = b.op("vector", lambda e: e.tensor_tensor(tv[0], x2, cb, ALU.mult),
                      inc="sdv")
            return b.op("vector", lambda e: e.tensor_sub(x2, tv[0], tv[1]),
                        swait=[("sdv", f1)], inc="sdv")

        wk_sb = alloc(16 * 1024, BF16)
        wv_sb = alloc(16 * 1024, BF16)
        rct = alloc(32 * 32, BF16)
        rst = alloc(32 * 32, BF16)
        xbf = [alloc(E, BF16) for _ in range(2)]
        xn = [alloc(E, BF16) for _ in range(2)]
        xnt_sb = [alloc(16 * 128, BF16) for _ in range(2)]
        dbgsnap = alloc(64, F32)
        dbgx = alloc(4 * E, BF16)
        ktm = alloc(1024, BF16)
        ktf = alloc(1024, BF16)
        rtmp = alloc(3 * 1024, BF16)
        vst = alloc(1024, BF16)

        _wsplit(b, "gpsimd", wk_sb.rearrange("p (g n) -> p g n", g=16),
                d["wk"].rearrange("(g p) n -> g p n", p=128), 4, wait=P0_done)
        _wsplit(b, "gpsimd", wv_sb.rearrange("p (g n) -> p g n", g=16),
                d["wv"].rearrange("(g p) n -> g p n", p=128), 4)
        b.dma("gpsimd", rct, d["rcos"].rearrange("p a b -> p (a b)"))
        wkv_ready = b.dma("gpsimd", rst, d["rsin"].rearrange("p a b -> p (a b)"))
        rc3 = rct.rearrange("p (a b) -> p a b", a=32)
        rs3 = rst.rearrange("p (a b) -> p a b", a=32)

        for kt in range(32):
            tb = kt % 2
            xt = xnt_sb[tb]
            war = b.all_now("spe") if kt >= 2 else list(P0_done)
            ld = b.dma("gpsimd", xbf[tb], d["xin"][kt * 128:(kt + 1) * 128, :],
                       wait=war)
            emit_ln(xbf[tb], xn[1 - tb], xn[tb], wait_ld=[("qg", ld)],
                    war_apply=b.all_now("spe"))
            xnw = b.now("sdv")
            for c in range(16):
                b.op("tensor", lambda e, c=c, tb=tb: e.matmul(
                    bank(c // 4, (c % 4) * 128, 128), xn[tb][:, c * 128:(c + 1) * 128],
                    identb, start=True, stop=True),
                    wait=([("sdv", xnw), ("qg", wkv_ready)]) if c == 0 else (),
                    inc="spe" if c == 15 else None)
            ev = b.op("vector", lambda e, xt=xt: e.tensor_copy(xt, ps[:, 0:2048]),
                      wait=b.all_now("spe") + [("qs", b.now("qs"))], inc="sdv")
            b.dma("sync", d["xnt"][kt], xt, wait=[("sdv", ev)])
            if dbg and kt < 4:
                b.op("vector", lambda e, kt=kt: e.tensor_copy(dbgsnap[:, kt * 16:(kt + 1) * 16], lnst))
                snp = b.op("vector", lambda e, kt=kt, tb=tb: e.tensor_copy(
                    dbgx[:, kt * E:(kt + 1) * E], xn[tb]), inc="sdv")
                b.dma("sync", d["dbg1"][kt], dbgx[:, kt * E:(kt + 1) * E], wait=[("sdv", snp)])
                b.dma("sync", d["dbg2"][kt], dbgsnap[:, kt * 16:(kt + 1) * 16], wait=[("sdv", snp)])
            for half in range(2):
                for ic in range(16):
                    b.op("tensor", lambda e, half=half, ic=ic, xt=xt: e.matmul(
                        bank(4 + half), xt[:, ic * 128:(ic + 1) * 128],
                        wk_sb[:, ic * 1024 + half * 512:ic * 1024 + (half + 1) * 512],
                        start=(ic == 0), stop=(ic == 15)),
                        wait=[("sdv", ev)] if (half == 0 and ic == 0) else (),
                        inc="spe" if (half == 1 and ic == 15) else None)
            b.op("vector", lambda e: e.tensor_copy(ktm, ps[:, 4 * 512:6 * 512]),
                 wait=b.all_now("spe"), inc="sdv")
            emit_rope_tm(ktm, 16, rc3[:, kt, :], rs3[:, kt, :], rtmp)
            krp = b.now("sdv")
            for c in range(8):
                b.op("tensor", lambda e, c=c: e.matmul(
                    bank(6 + c // 4, (c % 4) * 128, 128), ktm[:, c * 128:(c + 1) * 128],
                    identb, start=True, stop=True),
                    wait=[("sdv", krp)] if c == 0 else (),
                    inc="spe" if c == 7 else None)
            kev = b.op("vector", lambda e: e.tensor_copy(ktf, ps[:, 6 * 512:8 * 512]),
                       wait=b.all_now("spe") + [("qs", b.now("qs"))], inc="sdv")
            b.dma("sync", d["ka"][kt], ktf, wait=[("sdv", kev)])
            for half in range(2):
                for ic in range(16):
                    b.op("tensor", lambda e, half=half, ic=ic, xt=xt: e.matmul(
                        bank(4 + half), xt[:, ic * 128:(ic + 1) * 128],
                        wv_sb[:, ic * 1024 + half * 512:ic * 1024 + (half + 1) * 512],
                        start=(ic == 0), stop=(ic == 15)),
                        wait=[("sdv", b.now("sdv"))] if (half == 0 and ic == 0) else (),
                        inc="spe" if (half == 1 and ic == 15) else None)
            b.op("vector", lambda e: e.tensor_copy(vst, ps[:, 4 * 512:6 * 512]),
                 wait=b.all_now("spe") + [("qs", b.now("qs"))], inc="sdv")
            b.dma("sync", d["va"][kt], vst, wait=b.all_now("sdv"))
        P1a_done = b.all_now("sdv", "spe", "qs", "qg", "sac")
        phase_reset()

        if stop_after == "P1a":
            b.op("sync", lambda e: e.wait_ge(b.sems["qs"], b.now("qs")))
            b.op("sync", lambda e: e.wait_ge(b.sems["qg"], b.now("qg")))
            b.emit()
            return nc, d

        wq_sb = alloc(16 * E, BF16)
        rctq = alloc(20 * 32, BF16)
        rstq = alloc(20 * 32, BF16)
        xntq = [alloc(E, BF16) for _ in range(2)]
        qtm = alloc(E, BF16)
        qtf = alloc(E, BF16)
        rtmpQ = alloc(3 * 2048, BF16)
        _wsplit(b, "gpsimd", wq_sb.rearrange("p (g n) -> p g n", g=16),
                d["wq"].rearrange("(g p) n -> g p n", p=128), 4, wait=P1a_done)
        b.dma("gpsimd", rctq, d["rcos"][:, 12:32, :].rearrange("p a b -> p (a b)"))
        wq_ready = b.dma("gpsimd", rstq, d["rsin"][:, 12:32, :].rearrange("p a b -> p (a b)"))
        rcq3 = rctq.rearrange("p (a b) -> p a b", a=20)
        rsq3 = rstq.rearrange("p (a b) -> p a b", a=20)

        for qt in range(20):
            tb = qt % 2
            ld = b.dma("sync", xntq[tb], d["xnt"][12 + qt],
                       wait=(list(P1a_done) + [("sdv", b.now("sdv"))]) if qt < 2
                       else [("spe", b.now("spe"))])
            for oc4 in range(4):
                for ic in range(16):
                    b.op("tensor", lambda e, oc4=oc4, ic=ic, tb=tb: e.matmul(
                        bank(oc4), xntq[tb][:, ic * 128:(ic + 1) * 128],
                        wq_sb[:, ic * E + oc4 * 512:ic * E + (oc4 + 1) * 512],
                        start=(ic == 0), stop=(ic == 15)),
                        wait=[("qs", ld), ("qg", wq_ready), ("sdv", b.now("sdv"))]
                        if (oc4 == 0 and ic == 0) else (),
                        inc="spe" if (oc4 == 3 and ic == 15) else None)
            b.op("vector", lambda e: e.tensor_copy(qtm, ps[:, 0:2048]),
                 wait=b.all_now("spe"), inc="sdv")
            emit_rope_tm(qtm, 32, rcq3[:, qt, :], rsq3[:, qt, :], rtmpQ)
            qrp = b.now("sdv")
            for c in range(16):
                b.op("tensor", lambda e, c=c: e.matmul(
                    bank(4 + c // 4, (c % 4) * 128, 128), qtm[:, c * 128:(c + 1) * 128],
                    identb, start=True, stop=True),
                    wait=[("sdv", qrp)] if c == 0 else (),
                    inc="spe" if c == 15 else None)
            qev = b.op("vector", lambda e: e.tensor_copy(qtf, ps[:, 4 * 512:8 * 512]),
                       wait=b.all_now("spe") + [("qs", b.now("qs"))], inc="sdv")
            b.dma("sync", d["qa"][qt], qtf, wait=[("sdv", qev)])
        P1b_done = b.all_now("sdv", "spe", "qs", "qg")
        phase_reset()

        if stop_after == "P1b":
            b.op("sync", lambda e: e.wait_ge(b.sems["qs"], b.now("qs")))
            b.op("sync", lambda e: e.wait_ge(b.sems["qg"], b.now("qg")))
            b.emit()
            return nc, d

        ka_sb = alloc(32 * 1024, BF16)
        va_sb = alloc(32 * 1024, BF16)
        q_sb = [alloc(4 * E, BF16) for _ in range(2)]
        p_sb = [alloc(1024, BF16) for _ in range(2)]
        dmaskb = alloc(4 * 1024, BF16)
        l_sb = alloc(1024, F32, rows=1)
        lcol = alloc(16, F32)
        a0f = alloc(512, F32)
        a1f = alloc(512, F32)
        attf = alloc(128, F32)
        sqd = alloc(128, F32)
        anq = alloc(128, BF16)
        an_sb = alloc(512, BF16)
        rmss = alloc(8, F32)

        _wsplit(b, "gpsimd", ka_sb.rearrange("p (g n) -> p g n", g=32), d["ka"], 8,
                wait=P1b_done)
        _wsplit(b, "gpsimd", va_sb.rearrange("p (g n) -> p g n", g=32), d["va"], 8)
        kav_ready = b.dma("gpsimd", dmaskb, d["dmask"].rearrange("g p n -> p g n"))

        pe_qb_mark = {}
        for qb in range(QB_N):
            qsb = q_sb[qb % 2]
            qw = [("qg", kav_ready)]
            if qb >= 2:
                qw.append(("spe", pe_qb_mark[qb - 2]))
            qld = _wsplit(b, "sync", qsb.rearrange("p (g n) -> p g n", g=4),
                          d["qa"][qb * 4:(qb + 1) * 4], 2, wait=qw)
            q3 = qsb.rearrange("p (t c n) -> p t c n", t=4, c=16)
            nkt = KMAX[qb] // 128
            for dh in range(NDH):
                kc, base = dh // 2, 64 * (dh % 2)
                qc = 2 * (dh // 2)
                vcol = (dh // 2) * 128
                exp_mark = {}
                pe_pmark = {}
                mask_dv = {}

                def emit_lav(kt):
                    P = kt % 2
                    pb = p_sb[P]
                    wav = ([("sdv", mask_dv[kt])] if mask_dv.get(kt)
                           else [("sac", exp_mark[kt])])
                    b.op("tensor", lambda e, pb=pb, kt=kt: e.matmul(
                        bank(6, 0, 512)[0:1, :], onesb, pb[:, 0:512],
                        start=(kt == 0), stop=(kt == nkt - 1)), wait=wav)
                    b.op("tensor", lambda e, pb=pb, kt=kt: e.matmul(
                        bank(7, 0, 512)[0:1, :], onesb, pb[:, 512:1024],
                        start=(kt == 0), stop=(kt == nkt - 1)))
                    b.op("tensor", lambda e, pb=pb, kt=kt, vcol=vcol: e.matmul(
                        bank(4), va_sb[:, kt * 1024 + vcol:kt * 1024 + vcol + 128],
                        pb[:, 0:512], start=(kt == 0), stop=(kt == nkt - 1)))
                    pe_pmark[kt] = b.op("tensor", lambda e, pb=pb, kt=kt, vcol=vcol: e.matmul(
                        bank(5), va_sb[:, kt * 1024 + vcol:kt * 1024 + vcol + 128],
                        pb[:, 512:1024], start=(kt == 0), stop=(kt == nkt - 1)), inc="spe")

                for kt in range(nkt):
                    P = kt % 2
                    sb0 = P * 2
                    pb = p_sb[P]
                    w = [("qs", qld)]
                    if kt >= 2:
                        w.append(("sac", exp_mark[kt - 2]))
                    if kt == 0:
                        w.append(("sdv", b.now("sdv")))
                        w.append(("sac", b.now("sac")))
                    b.op("tensor", lambda e, kc=kc, base=base, kt=kt, sb0=sb0, qc=qc, q3=q3: e.matmul(
                        bank(sb0),
                        ka_sb[base:base + 64, kt * 1024 + kc * 128:kt * 1024 + (kc + 1) * 128],
                        q3[base:base + 64, :, qc, :],
                        start=True, stop=True), wait=w)
                    b.op("tensor", lambda e, kc=kc, base=base, kt=kt, sb0=sb0, qc=qc, q3=q3: e.matmul(
                        bank(sb0 + 1),
                        ka_sb[base:base + 64, kt * 1024 + kc * 128:kt * 1024 + (kc + 1) * 128],
                        q3[base:base + 64, :, qc + 1, :],
                        start=True, stop=True), inc="spe")
                    smm = b.now("spe")
                    wexp = [("spe", smm)]
                    if kt >= 2:
                        wexp.append(("spe", pe_pmark[kt - 2]))
                    exp_mark[kt] = b.op("scalar", lambda e, sb0=sb0, pb=pb: e.activation(
                        pb, ps[:, sb0 * 512:(sb0 + 2) * 512], AF.Exp, scale=SCALE_A),
                        wait=wexp, inc="sac")
                    o = kt - (qb * 4 + 12)
                    if 0 <= o <= 3:
                        mask_dv[kt] = b.op("vector", lambda e, pb=pb, o=o: e.tensor_mul(
                            pb, pb, dmaskb[:, o * 1024:(o + 1) * 1024]),
                            wait=[("sac", exp_mark[kt])], inc="sdv")
                    if kt >= 1:
                        emit_lav(kt - 1)
                emit_lav(nkt - 1)
                kdone = b.now("spe")
                b.op("vector", lambda e: e.tensor_copy(l_sb, ps[0:1, 6 * 512:8 * 512]),
                     wait=[("spe", kdone)])
                b.op("vector", lambda e: e.tensor_copy(a0f, bank(4)))
                aev = b.op("vector", lambda e: e.tensor_copy(a1f, bank(5)), inc="sdv")
                for s in range(2):
                    for q4 in range(4):
                        b.op("tensor", lambda e, s=s, q4=q4: e.matmul(
                            bank(2, s * 4 + q4, 1),
                            l_sb[0:1, s * 512 + q4 * 128:s * 512 + (q4 + 1) * 128],
                            onesf[0:1, 0:1], start=True, stop=True),
                            wait=[("sdv", aev)] if (s == 0 and q4 == 0) else (),
                            inc="spe" if (s == 1 and q4 == 3) else None)
                lmm = b.now("spe")
                g0 = b.op("vector", lambda e: e.tensor_copy(lcol[:, 0:8], bank(2, 0, 8)),
                          wait=[("spe", lmm)], inc="sdv")
                b.op("vector", lambda e, qb=qb: e.tensor_sub(
                    lcol[:, 0:4], lcol[:, 0:4], npadc[:, qb * 4:(qb + 1) * 4]),
                    swait=[("sdv", g0)])
                g1 = b.op("vector", lambda e, qb=qb: e.tensor_sub(
                    lcol[:, 4:8], lcol[:, 4:8], npadc[:, qb * 4:(qb + 1) * 4]), inc="sdv")
                g2 = b.op("vector", lambda e: e.reciprocal(lcol[:, 8:16], lcol[:, 0:8]),
                          swait=[("sdv", g1)], inc="sdv")
                b.op("vector", lambda e: e.tensor_tensor(
                    lcol[:, 4:8], lcol[:, 12:16], nlam.broadcast_to([128, 4]), ALU.mult),
                    swait=[("sdv", g2)], inc="sdv")
                linv = b.now("sdv")
                for q4 in range(4):
                    b.op("tensor", lambda e, q4=q4: e.transpose(
                        bank(0, q4 * 128, 128), a0f[:, q4 * 128:(q4 + 1) * 128], identf),
                        wait=[("sdv", aev), ("sac", b.now("sac"))] if q4 == 0 else ())
                    b.op("tensor", lambda e, q4=q4: e.transpose(
                        bank(1, q4 * 128, 128), a1f[:, q4 * 128:(q4 + 1) * 128], identf),
                        inc="spe" if q4 == 3 else None)
                tdone = b.now("spe")
                sqm = 0
                btm = 0
                for q4 in range(4):
                    ww = [("spe", tdone), ("sdv", linv)]
                    if sqm:
                        ww.append(("sac", sqm))
                    b.op("vector", lambda e, q4=q4: e.tensor_scalar(
                        attf, bank(0, q4 * 128, 128), lcol[:, 8 + q4:9 + q4], None, ALU.mult),
                        wait=ww, swait=[("sdv", linv)])
                    b.op("vector", lambda e, q4=q4: e.scalar_tensor_tensor(
                        attf, bank(1, q4 * 128, 128), lcol[:, 4 + q4:5 + q4], attf,
                        ALU.mult, ALU.add), inc="sdv")
                    sqm = b.op("scalar", lambda e: e.activation(sqd, attf, AF.Square,
                                                                accum_out=rmss[:, 0:1]),
                               wait=b.all_now("sdv"), inc="sac")
                    b.op("vector", lambda e: e.tensor_scalar(
                        rmss[:, 1:2], rmss[:, 0:1], 1.0 / VD, RMSEPS, ALU.mult, ALU.add),
                        wait=[("sac", sqm)], inc="sdv")
                    b.op("scalar", lambda e: e.activation(rmss[:, 2:3], rmss[:, 1:2], AF.Sqrt),
                         wait=b.all_now("sdv"), inc="sac")
                    r3 = b.op("vector", lambda e: e.reciprocal(rmss[:, 3:4], rmss[:, 2:3]),
                              wait=b.all_now("sac"), inc="sdv")
                    wanq = [("spe", btm)] if btm else []
                    b.op("vector", lambda e: e.scalar_tensor_tensor(
                        anq, attf, rmss[:, 3:4], geff, ALU.mult, ALU.mult),
                        wait=wanq, swait=[("sdv", r3)], inc="sdv")
                    btm = b.op("tensor", lambda e, q4=q4: e.matmul(
                        bank(3, q4 * 128, 128), anq, identb, start=True, stop=True),
                        wait=b.all_now("sdv"), inc="spe")
                ev2 = b.op("vector", lambda e: e.tensor_copy(an_sb, bank(3)),
                           wait=[("spe", btm), ("qs", b.now("qs"))], inc="sdv")
                b.dma("sync", d["anorm"][dh][:, qb * 512:(qb + 1) * 512], an_sb,
                      wait=[("sdv", ev2)])
            pe_qb_mark[qb] = b.now("spe")
        P2_done = b.all_now("sdv", "spe", "qs", "qg", "sac")
        phase_reset()

        if stop_after == "P2":
            b.op("sync", lambda e: e.wait_ge(b.sems["qs"], b.now("qs")))
            b.op("sync", lambda e: e.wait_ge(b.sems["qg"], b.now("qg")))
            b.emit()
            return nc, d

        wo_sb = alloc(16 * E, BF16)
        an16 = [alloc(16 * 512, BF16) for _ in range(2)]
        xint = [alloc(E, F32) for _ in range(2)]
        xrout = [alloc(E, F32) for _ in range(2)]
        _wsplit(b, "gpsimd", wo_sb.rearrange("p (g n) -> p g n", g=16),
                d["wo"].rearrange("(g p) n -> g p n", p=128), 4, wait=P2_done)
        wo_ready = b.now("qg")
        qb_pe = {}
        for qb in range(QB_N):
            sb = qb % 2
            aw = list(P2_done)
            if qb >= 2:
                aw.append(("spe", qb_pe[qb - 2]))
            anl = _wsplit(b, "sync", an16[sb].rearrange("p (g n) -> p g n", g=16),
                          d["anorm"][:, :, qb * 512:(qb + 1) * 512], 4, wait=aw)
            for q4 in range(4):
                tb = q4 % 2
                qt = qb * 4 + q4
                xl = b.dma("sync", xint[tb], d["xin"][(12 + qt) * 128:(13 + qt) * 128, :],
                           wait=b.all_now("sdv"))
                for do in range(4):
                    for dh in range(16):
                        b.op("tensor", lambda e, do=do, dh=dh, q4=q4, sb=sb: e.matmul(
                            bank(do), an16[sb][:, dh * 512 + q4 * 128:dh * 512 + (q4 + 1) * 128],
                            wo_sb[:, dh * E + do * 512:dh * E + (do + 1) * 512],
                            start=(dh == 0), stop=(dh == 15)),
                            wait=[("qs", anl), ("qg", wo_ready), ("sdv", b.now("sdv"))]
                            if (do == 0 and dh == 0) else (),
                            inc="spe" if (do == 3 and dh == 15) else None)
                mm = b.now("spe")
                ra = b.op("vector", lambda e, tb=tb: e.tensor_add(
                    xrout[tb], xint[tb], ps[:, 0:2048]),
                    wait=[("spe", mm), ("qs", xl)], inc="sdv")
                b.dma("sync", d["xres"][qt * 128:(qt + 1) * 128, :], xrout[tb],
                      wait=[("sdv", ra)])
            qb_pe[qb] = b.now("spe")
        P2b_done = b.all_now("sdv", "spe", "qs", "qg")
        phase_reset()

        if stop_after == "P2b":
            b.op("sync", lambda e: e.wait_ge(b.sems["qs"], b.now("qs")))
            b.op("sync", lambda e: e.wait_ge(b.sems["qg"], b.now("qg")))
            b.emit()
            return nc, d

        def emit_ffn_group(g0, gn, to_xres, wait0):
            nt = gn // 128
            xnt_g = alloc(16 * gn, BF16)
            wgb = [alloc(16 * 512, BF16) for _ in range(2)]
            wub = [alloc(16 * 512, BF16) for _ in range(2)]
            wdb = alloc(4 * 2048, BF16)
            acc = alloc(nt * 2048, F32)
            xint2 = [alloc(E, F32) for _ in range(2)]
            xn2 = [alloc(E, BF16) for _ in range(2)]
            sg2 = [alloc(512, BF16) for _ in range(2)]
            hb2 = [alloc(512, BF16) for _ in range(2)]
            hsp2 = [alloc(512, BF16) for _ in range(2)]
            hspT2 = [alloc(512, BF16) for _ in range(2)]
            sp3 = [alloc(512, F32) for _ in range(3)]

            ln_ev = []
            for t in range(nt):
                tb = t % 2
                war = [("sdv", ln_ev[-2])] if len(ln_ev) >= 2 else list(wait0)
                ld = b.dma("sync", xint2[tb], d["xres"][g0 + t * 128:g0 + (t + 1) * 128, :],
                           wait=war)
                emit_ln(xint2[tb], xn2[1 - tb], xn2[tb], wait_ld=[("qs", ld)],
                        war_apply=b.all_now("spe"))
                xnw = b.now("sdv")
                b0 = (t % 2) * 4
                for c in range(16):
                    bk = b0 + c // 4
                    b.op("tensor", lambda e, c=c, bk=bk, tb=tb: e.matmul(
                        bank(bk, (c % 4) * 128, 128), xn2[tb][:, c * 128:(c + 1) * 128],
                        identb, start=True, stop=True),
                        wait=([("sdv", xnw)] + ([("sdv", ln_ev[-1])] if ln_ev else []))
                        if c == 0 else (), inc="spe" if c == 15 else None)
                ev = b.op("vector", lambda e, t=t, b0=b0: e.tensor_copy(
                    xnt_g[:, t * 2048:(t + 1) * 2048], ps[:, b0 * 512:(b0 + 4) * 512]),
                    wait=b.all_now("spe"), inc="sdv")
                ln_ev.append(ev)
            ln_done = b.now("sdv")
            xg3 = xnt_g.rearrange("p (t c n) -> p t c n", t=nt, c=16)

            silu_mark = {}
            evT_mark = {}
            acc_mark = {}
            pool_mark = {}
            trm_mark = {}
            pe_chunk = {}
            for ic in range(NIC):
                buf = ic % 2
                wwait = [("sdv", ln_done)]
                if ic >= 2:
                    wwait.append(("spe", pe_chunk[ic - 2]))
                _wsplit(b, "gpsimd", wgb[buf].rearrange("p (g n) -> p g n", g=16),
                        d["wg"][:, ic * 512:(ic + 1) * 512].rearrange("(g p) n -> g p n", p=128),
                        4, wait=wwait)
                _wsplit(b, "gpsimd", wub[buf].rearrange("p (g n) -> p g n", g=16),
                        d["wu"][:, ic * 512:(ic + 1) * 512].rearrange("(g p) n -> g p n", p=128), 4)
                wdv3 = wdb.rearrange("p (g n) -> p g n", g=4)
                wdw = [("spe", pe_chunk[ic - 1])] if ic >= 1 else ()
                wld0 = _wsplit(b, "gpsimd", wdv3[:, :, 0:1024],
                               d["wd"][ic * 512:(ic + 1) * 512, 0:1024].rearrange("(g p) n -> g p n", p=128),
                               2, wait=wdw)
                wld = _wsplit(b, "gpsimd", wdv3[:, :, 1024:2048],
                              d["wd"][ic * 512:(ic + 1) * 512, 1024:2048].rearrange("(g p) n -> g p n", p=128), 2)
                for t in range(nt):
                    tb = t % 2
                    pA = (t % 2) * 2
                    key = (ic, t)
                    pkey = (ic, t - 2) if t >= 2 else ((ic - 1, nt - 2 + t) if ic >= 1 else None)
                    w = [("qg", wld0)]
                    if pkey is not None and pkey in evT_mark:
                        w.append(("sdv", evT_mark[pkey]))
                    for rc in range(16):
                        b.op("tensor", lambda e, rc=rc, t=t, pA=pA, buf=buf: e.matmul(
                            bank(pA), xg3[:, t, rc, :], wgb[buf][:, rc * 512:(rc + 1) * 512],
                            start=(rc == 0), stop=(rc == 15)), wait=w if rc == 0 else ())
                        b.op("tensor", lambda e, rc=rc, t=t, pA=pA, buf=buf: e.matmul(
                            bank(pA + 1), xg3[:, t, rc, :], wub[buf][:, rc * 512:(rc + 1) * 512],
                            start=(rc == 0), stop=(rc == 15)),
                            inc="spe" if rc == 15 else None)
                    gmm = b.now("spe")
                    wsilu = [("spe", gmm)]
                    if pkey is not None and pkey in evT_mark:
                        wsilu.append(("sdv", evT_mark[pkey]))
                    silu_mark[key] = b.op("scalar", lambda e, tb=tb, pA=pA: e.activation(
                        sg2[tb], bank(pA), AF.Silu), wait=wsilu, inc="sac")
                    wh = [("sac", silu_mark[key])]
                    fh = b.op("vector", lambda e, tb=tb, pA=pA: e.tensor_tensor(
                        hb2[tb], sg2[tb], bank(pA + 1), ALU.mult),
                        wait=wh, inc="sdv")
                    h4 = hb2[tb].rearrange("p (n f) -> p n f", f=4)
                    hs4 = hsp2[tb].rearrange("p (n f) -> p n f", f=4)
                    a = lambda i: sp3[0][:, i * 128:(i + 1) * 128]
                    mm_ = lambda i: sp3[1][:, i * 128:(i + 1) * 128]
                    z = lambda i: sp3[2][:, i * 128:(i + 1) * 128]
                    wp = []
                    if pkey is not None and pkey in trm_mark:
                        wp.append(("spe", trm_mark[pkey]))
                    for i in range(4):
                        b.op("vector", lambda e, i=i, h4=h4, a=a: e.tensor_mul(
                            a(i), h4[:, :, i], h4[:, :, i]),
                            wait=wp if i == 0 else (), swait=[("sdv", fh)] if i == 0 else (),
                            inc="sdv" if i == 3 else None)
                    fa = b.now("sdv")
                    b.op("vector", lambda e, a=a, m=mm_: e.tensor_tensor(m(0), a(0), a(1), ALU.min),
                         swait=[("sdv", fa)])
                    b.op("vector", lambda e, a=a, m=mm_: e.tensor_tensor(m(1), a(2), a(3), ALU.min))
                    b.op("vector", lambda e, a=a, m=mm_: e.tensor_tensor(m(2), a(0), a(1), ALU.max))
                    fm = b.op("vector", lambda e, a=a, m=mm_: e.tensor_tensor(m(3), a(2), a(3), ALU.max),
                              inc="sdv")
                    b.op("vector", lambda e, m=mm_, z=z: e.tensor_tensor(z(0), m(0), m(1), ALU.max),
                         swait=[("sdv", fm)])
                    fz = b.op("vector", lambda e, m=mm_, z=z: e.tensor_tensor(z(1), m(2), m(3), ALU.min),
                              inc="sdv")
                    fz2 = b.op("vector", lambda e, z=z: e.tensor_tensor(z(2), z(0), z(1), ALU.max),
                               swait=[("sdv", fz)], inc="sdv")
                    for i in range(4):
                        b.op("vector", lambda e, i=i, a=a, m=mm_, z=z: e.tensor_tensor(
                            m(i), a(i), z(2), ALU.is_ge),
                            swait=[("sdv", fz2)] if i == 0 else (),
                            inc="sdv" if i == 3 else None)
                    fk = b.now("sdv")
                    prev = 0
                    for i in range(4):
                        prev = b.op("vector", lambda e, i=i, h4=h4, hs4=hs4, m=mm_: e.tensor_tensor(
                            hs4[:, :, i], h4[:, :, i], m(i), ALU.mult),
                            swait=[("sdv", fk)] if i == 0 else (),
                            inc="sdv" if i == 3 else None)
                    pool_mark[key] = fk
                    hm = prev
                    for s in range(4):
                        b.op("tensor", lambda e, s=s, tb=tb, pA=pA: e.matmul(
                            bank(pA, s * 128, 128), hsp2[tb][:, s * 128:(s + 1) * 128],
                            identb, start=True, stop=True),
                            wait=[("sdv", hm)] if s == 0 else (),
                            inc="spe" if s == 3 else None)
                    trm = b.now("spe")
                    trm_mark[key] = trm
                    evT_mark[key] = b.op("vector", lambda e, tb=tb, pA=pA: e.tensor_copy(
                        hspT2[tb], bank(pA)), wait=[("spe", trm)], inc="sdv")
                    wdown = [("sdv", evT_mark[key]), ("qg", wld0)]
                    if pkey is not None and pkey in acc_mark:
                        wdown.append(("sdv", acc_mark[pkey]))
                    for o in range(4):
                        for s in range(4):
                            wv = wdown if (s == 0 and o == 0) else (
                                [("qg", wld)] if (s == 0 and o == 2) else ())
                            b.op("tensor", lambda e, s=s, o=o, tb=tb: e.matmul(
                                bank(4 + o), hspT2[tb][:, s * 128:(s + 1) * 128],
                                wdb[:, s * 2048 + o * 512:s * 2048 + (o + 1) * 512],
                                start=(s == 0), stop=(s == 3)),
                                wait=wv,
                                inc="spe" if (s == 3 and o == 3) else None)
                    dmm = b.now("spe")
                    r = 0
                    for o in range(4):
                        r = b.op("vector", lambda e, o=o, t=t, ic=ic: (
                            e.tensor_copy(acc[:, t * 2048 + o * 512:t * 2048 + (o + 1) * 512],
                                          bank(4 + o))
                            if ic == 0 else
                            e.tensor_add(acc[:, t * 2048 + o * 512:t * 2048 + (o + 1) * 512],
                                         acc[:, t * 2048 + o * 512:t * 2048 + (o + 1) * 512],
                                         bank(4 + o))),
                            wait=[("spe", dmm)] if o == 0 else (),
                            inc="sdv")
                    acc_mark[key] = r
                pe_chunk[ic] = b.now("spe")
            for t in range(nt):
                tb = t % 2
                xl = b.dma("sync", xint2[tb], d["xres"][g0 + t * 128:g0 + (t + 1) * 128, :],
                           wait=[("sdv", acc_mark[(NIC - 1, min(t, nt - 1))]),
                                 ("spe", b.now("spe"))])
                ra = b.op("vector", lambda e, tb=tb, t=t: e.tensor_add(
                    xint2[tb], xint2[tb], acc[:, t * 2048:(t + 1) * 2048]),
                    wait=[("qs", xl)], inc="sdv")
                if to_xres:
                    b.dma("sync", d["xres"][g0 + t * 128:g0 + (t + 1) * 128, :], xint2[tb],
                          wait=[("sdv", ra)])
                else:
                    b.dma("sync", d["out"][g0 - PADQ + t * 128:g0 - PADQ + (t + 1) * 128, :],
                          xint2[tb], wait=[("sdv", ra)])
            r = b.all_now("sdv", "spe", "qs", "qg", "sac")
            phase_reset()
            return r

        def emit_ffn(row0, nrows, to_xres, wait0):
            done = wait0
            g0 = row0
            while g0 < row0 + nrows:
                gn = min(896, row0 + nrows - g0)
                done = emit_ffn_group(g0, gn, to_xres, done)
                g0 += gn
            return done

        f1_done = emit_ffn(0, RE, True, P2b_done)
        if stop_after == "P3":
            b.op("sync", lambda e: e.wait_ge(b.sems["qs"], b.now("qs")))
            b.op("sync", lambda e: e.wait_ge(b.sems["qg"], b.now("qg")))
            b.emit()
            return nc, d

        kb_sb = alloc(4 * RE, BF16)
        vb_sb = alloc(20 * 512, BF16)
        qfm = alloc(16 * OWN, BF16)
        P4_FLOOR = cur[0]

        bwk_sb = alloc(16 * 512, BF16)
        bwv_sb = alloc(16 * 512, BF16)
        xint3 = [alloc(E, F32) for _ in range(2)]
        xn3 = [alloc(E, BF16) for _ in range(2)]
        xnt3_sb = [alloc(16 * 512, BF16) for _ in range(2)]
        _wsplit(b, "gpsimd", bwk_sb.rearrange("p (g n) -> p g n", g=16),
                d["bwk"].rearrange("(g p) n -> g p n", p=128), 4, wait=f1_done)
        bw_ready = _wsplit(b, "gpsimd", bwv_sb.rearrange("p (g n) -> p g n", g=16),
                           d["bwv"].rearrange("(g p) n -> g p n", p=128), 4)
        ev3 = []
        for kb5 in range(5):
            xt = xnt3_sb[kb5 % 2]
            for t4 in range(4):
                rt = kb5 * 4 + t4
                tb = rt % 2
                war = [("sdv", ev3[-2])] if len(ev3) >= 2 else list(f1_done)
                ld = b.dma("sync", xint3[tb], d["xres"][rt * 128:(rt + 1) * 128, :],
                           wait=war)
                emit_ln(xint3[tb], xn3[1 - tb], xn3[tb], wait_ld=[("qs", ld)],
                        war_apply=b.all_now("spe"))
                xnw = b.now("sdv")
                b0 = (t4 % 2) * 4
                for c in range(16):
                    bk = b0 + c // 4
                    b.op("tensor", lambda e, c=c, bk=bk, tb=tb: e.matmul(
                        bank(bk, (c % 4) * 128, 128), xn3[tb][:, c * 128:(c + 1) * 128],
                        identb, start=True, stop=True),
                        wait=([("sdv", xnw), ("qg", bw_ready)]
                              + ([("sdv", ev3[-1])] if ev3 else [])) if c == 0 else (),
                        inc="spe" if c == 15 else None)
                ev = b.op("vector", lambda e, t4=t4, b0=b0, xt=xt: e.tensor_copy(
                    xt[:, t4 * 2048:(t4 + 1) * 2048], ps[:, b0 * 512:(b0 + 4) * 512]),
                    wait=b.all_now("spe") + [("qs", b.now("qs"))], inc="sdv")
                ev3.append(ev)
                b.dma("sync", d["xntb"][rt], xt[:, t4 * 2048:(t4 + 1) * 2048],
                      wait=[("sdv", ev)])
            x3 = xt.rearrange("p (t c n) -> p t c n", t=4, c=16)
            kw = b.now("sdv")
            for kc in range(4):
                for ic in range(16):
                    b.op("tensor", lambda e, kc=kc, ic=ic, x3=x3: e.matmul(
                        bank(kc), bwk_sb[:, ic * 512 + kc * 128:ic * 512 + (kc + 1) * 128],
                        x3[:, :, ic, :], start=(ic == 0), stop=(ic == 15)),
                        wait=[("sdv", kw)] if (kc == 0 and ic == 0) else (),
                        inc="spe" if (kc == 3 and ic == 15) else None)
            kmm = b.now("spe")
            for kc in range(4):
                b.op("vector", lambda e, kc=kc, kb5=kb5: e.tensor_copy(
                    kb_sb[:, kc * RE + kb5 * 512:kc * RE + (kb5 + 1) * 512], bank(kc)),
                    wait=[("spe", kmm)] if kc == 0 else (), inc="sdv" if kc == 3 else None)
            for t4 in range(4):
                rt = kb5 * 4 + t4
                pbk = 4 + (t4 % 2)
                for ic in range(16):
                    b.op("tensor", lambda e, t4=t4, ic=ic, pbk=pbk, xt=xt: e.matmul(
                        bank(pbk), xt[:, (t4 * 16 + ic) * 128:(t4 * 16 + ic + 1) * 128],
                        bwv_sb[:, ic * 512:(ic + 1) * 512],
                        start=(ic == 0), stop=(ic == 15)),
                        wait=[("sdv", b.now("sdv"))] if ic == 0 else (),
                        inc="spe" if ic == 15 else None)
                b.op("vector", lambda e, rt=rt, pbk=pbk: e.tensor_copy(
                    vb_sb[:, rt * 512:(rt + 1) * 512], bank(pbk)),
                    wait=b.all_now("spe"), inc="sdv")
        P4a_done = b.all_now("sdv", "spe", "qs", "qg", "sac")
        phase_reset(P4_FLOOR)

        bwq_sb = alloc(16 * E, BF16)
        xntq3 = alloc(4 * E, BF16)
        _wsplit(b, "gpsimd", bwq_sb.rearrange("p (g n) -> p g n", g=16),
                d["bwq"].rearrange("(g p) n -> g p n", p=128), 4, wait=P4a_done)
        bq_ready = b.now("qg")
        for qb4 in range(4):
            lds = 0
            for t4 in range(4):
                rt = 4 + qb4 * 4 + t4
                lds = b.dma("sync", xntq3[:, t4 * E:(t4 + 1) * E], d["xntb"][rt],
                            wait=list(P4a_done) + [("spe", b.now("spe"))])
            xq3b = xntq3.rearrange("p (t c n) -> p t c n", t=4, c=16)
            for half in range(2):
                for oc in range(8):
                    occ = half * 8 + oc
                    for ic in range(16):
                        b.op("tensor", lambda e, occ=occ, oc=oc, ic=ic, xq3b=xq3b: e.matmul(
                            bank(oc), bwq_sb[:, ic * E + occ * 128:ic * E + occ * 128 + 128],
                            xq3b[:, :, ic, :], start=(ic == 0), stop=(ic == 15)),
                            wait=[("qs", lds), ("qg", bq_ready), ("sdv", b.now("sdv"))]
                            if (oc == 0 and ic == 0) else (),
                            inc="spe" if (oc == 7 and ic == 15) else None)
                qmm = b.now("spe")
                for oc in range(8):
                    occ = half * 8 + oc
                    b.op("vector", lambda e, occ=occ, oc=oc, qb4=qb4: e.tensor_copy(
                        qfm[:, occ * OWN + qb4 * 512:occ * OWN + (qb4 + 1) * 512], bank(oc)),
                        wait=[("spe", qmm)] if oc == 0 else (),
                        inc="sdv" if oc == 7 else None)
        P4b_done = b.all_now("sdv", "spe", "qs", "qg")
        phase_reset(P4_FLOOR)

        bwo_sb = alloc(16 * E, BF16)
        bm_sb = [alloc(5 * 128, BF16) for _ in range(2)]
        pb_sb = [alloc(640, BF16) for _ in range(2)]
        abf = alloc(16 * 128, BF16)
        lrow = alloc(16 * 128, F32, rows=1)
        linv = alloc(16 * 128, F32, rows=1)
        xint4_t = alloc(E, F32)
        xint4 = [xint4_t, xint4_t]
        _wsplit(b, "gpsimd", bwo_sb.rearrange("p (g n) -> p g n", g=16),
                d["bwo"].rearrange("(g p) n -> g p n", p=128), 4, wait=P4b_done)
        bo_ready = b.now("qg")
        for qt in range(16):
            mb = qt % 2
            bml = b.dma("sync", bm_sb[mb].rearrange("p (g n) -> p g n", g=5),
                        d["bmask"][qt].rearrange("g p n -> p g n"),
                        wait=b.all_now("sdv") + [("spe", b.now("spe"))])
            xl = b.dma("sync", xint4[mb], d["xres"][(4 + qt) * 128:(5 + qt) * 128, :],
                       wait=b.all_now("sdv") + [("qs", b.now("qs"))])
            abf_dv = 0
            for hh in range(HB):
                kvh = hh // 4
                pb = pb_sb[hh % 2]
                for et in range(5):
                    bk, off = (0, et * 128) if et < 4 else (1, 0)
                    w = []
                    if et == 0:
                        w = [("qs", bml), ("qg", bo_ready), ("sac", b.now("sac")),
                             ("sdv", b.now("sdv"))]
                    b.op("tensor", lambda e, kvh=kvh, qt=qt, et=et, bk=bk, off=off, hh=hh: e.matmul(
                        bank(bk, off, 128),
                        kb_sb[:, kvh * RE + (qt + et) * 128:kvh * RE + (qt + et + 1) * 128],
                        qfm[:, hh * OWN + qt * 128:hh * OWN + (qt + 1) * 128],
                        start=True, stop=True), wait=w, inc="spe" if et == 4 else None)
                smm = b.now("spe")
                em = b.op("scalar", lambda e, pb=pb: e.activation(
                    pb, ps[:, 0:640], AF.Exp, scale=SCALE_B),
                    wait=[("spe", smm), ("spe", b.now("spe"))], inc="sac")
                mk = b.op("vector", lambda e, pb=pb, mb=mb: e.tensor_mul(pb, pb, bm_sb[mb]),
                          wait=[("sac", em)], inc="sdv")
                for et in range(5):
                    b.op("tensor", lambda e, pb=pb, et=et: e.matmul(
                        bank(2, 0, 128)[0:1, :], onesb, pb[:, et * 128:(et + 1) * 128],
                        start=(et == 0), stop=(et == 4)),
                        wait=[("sdv", mk)] if et == 0 else ())
                for et in range(5):
                    b.op("tensor", lambda e, pb=pb, et=et, qt=qt, kvh=kvh: e.matmul(
                        bank(3, 0, 128), vb_sb[:, (qt + et) * 512 + kvh * 128:(qt + et) * 512 + (kvh + 1) * 128],
                        pb[:, et * 128:(et + 1) * 128], start=(et == 0), stop=(et == 4)),
                        inc="spe" if et == 4 else None)
                avm = b.now("spe")
                b.op("vector", lambda e, hh=hh: e.tensor_copy(
                    abf[:, hh * 128:(hh + 1) * 128], bank(3, 0, 128)), wait=[("spe", avm)])
                abf_dv = b.op("vector", lambda e, hh=hh: e.tensor_copy(
                    lrow[0:1, hh * 128:(hh + 1) * 128], bank(2, 0, 128)[0:1, :]),
                    inc="sdv")
            b.op("vector", lambda e: e.reciprocal(linv, lrow), swait=[("sdv", abf_dv)],
                 inc="sdv")
            rec = b.now("sdv")
            for hh in range(16):
                bk = 4 + hh // 4
                b.op("tensor", lambda e, hh=hh, bk=bk: e.matmul(
                    bank(bk, (hh % 4) * 128, 128), onesrow,
                    linv[0:1, hh * 128:(hh + 1) * 128], start=True, stop=True),
                    wait=[("sdv", rec)] if hh == 0 else (),
                    inc="spe" if hh == 15 else None)
            bcm = b.now("spe")
            sc = 0
            for hh in range(16):
                bk = 4 + hh // 4
                sc = b.op("vector", lambda e, hh=hh, bk=bk: e.tensor_tensor(
                    abf[:, hh * 128:(hh + 1) * 128], abf[:, hh * 128:(hh + 1) * 128],
                    bank(bk, (hh % 4) * 128, 128), ALU.mult),
                    wait=[("spe", bcm)] if hh == 0 else (), inc="sdv")
            for do in range(4):
                for hh in range(16):
                    b.op("tensor", lambda e, do=do, hh=hh: e.matmul(
                        bank(4 + do), abf[:, hh * 128:(hh + 1) * 128],
                        bwo_sb[:, hh * E + do * 512:hh * E + (do + 1) * 512],
                        start=(hh == 0), stop=(hh == 15)),
                        wait=[("sdv", sc)] if (do == 0 and hh == 0) else (),
                        inc="spe" if (do == 3 and hh == 15) else None)
            wmm = b.now("spe")
            ra = b.op("vector", lambda e, mb=mb: e.tensor_add(
                xint4[mb], xint4[mb], ps[:, 4 * 512:8 * 512]),
                wait=[("spe", wmm), ("qs", xl)], inc="sdv")
            b.dma("sync", d["xres"][(4 + qt) * 128:(5 + qt) * 128, :], xint4[mb],
                  wait=[("sdv", ra)])
        P4_done = b.all_now("sdv", "spe", "qs", "qg", "sac")
        phase_reset()

        if stop_after == "P4":
            b.op("sync", lambda e: e.wait_ge(b.sems["qs"], b.now("qs")))
            b.op("sync", lambda e: e.wait_ge(b.sems["qg"], b.now("qg")))
            b.emit()
            return nc, d

        emit_ffn(PADQ, OWN, False, P4_done)

        b.op("sync", lambda e: e.wait_ge(b.sems["qs"], b.now("qs")))
        b.op("sync", lambda e: e.wait_ge(b.sems["qg"], b.now("qg")))
        b.emit()
    return nc, d


def kernel(**inputs):
    maps = stage_inputs(inputs)
    nc, _ = build_program()
    res = run_bass_kernel_spmd(nc, maps, core_ids=list(range(NCORE)))
    out = np.empty((B, T, E), dtype=np.float32)
    for c in range(NCORE):
        bb, h = c // 2, c % 2
        out[bb, h * 2048:(h + 1) * 2048, :] = res.results[c]["out"]
    return out


# version 11
# speedup vs baseline: 1.1334x; 1.1334x over previous
import sys
import numpy as np

sys.path.insert(0, "/opt/trn_rl_repo")

import ml_dtypes
import concourse.bass as bass
import concourse.mybir as mybir
from concourse.bass_utils import run_bass_kernel_spmd
from contextlib import ExitStack

F32 = mybir.dt.float32
BF16 = mybir.dt.bfloat16
U8 = mybir.dt.uint8
AF = mybir.ActivationFunctionType
ALU = mybir.AluOpType
AX = mybir.AxisListType.X
bfdt = ml_dtypes.bfloat16

E = 2048
T = 4096
B = 4
NCORE = 8
RE = 2560
OWN = 2048
PADQ = 512
KEYS = 4096
LNEPS = 1e-5
HD = 64
NDH = 16
VD = 128
SCALE_A = HD ** -0.5
LAMBDA_INIT = 0.8 - 0.6 * float(np.exp(-0.3 * 12))
RMSEPS = 1e-6
HB = 16
HDB = 128
WIN = 512
SCALE_B = HDB ** -0.5
FI = 8192
NIC = 16
QB_N = 5
KMAX = [min(KEYS, qb * 512 + 2048) for qb in range(QB_N)]

S_E = [4 * m + s for m in range(8) for s in (0, 1)]
S_O = [4 * m + 2 + s for m in range(8) for s in (0, 1)]


def _qperm():
    cols = []
    for c in range(16):
        for s in (S_E[c], S_O[c]):
            cols.extend(range(s * HD, (s + 1) * HD))
    return np.array(cols)


def _npad_host(h):
    q = np.arange(RE)
    if h == 1:
        v = np.zeros(RE)
    else:
        sched = q + 1537.0
        v = np.where(q < PADQ, sched - 1.0, np.minimum(2048.0, sched))
    return np.ascontiguousarray(v.reshape(20, 128).T.astype(np.float32))


def _dmask_host():
    m = np.zeros((4, 128, 1024), dtype=bfdt)
    i = np.arange(128)[:, None]
    j = np.arange(512)[None, :]
    for o in range(4):
        mm = (i <= j - 128 * o).astype(bfdt)
        m[o, :, 0:512] = mm
        m[o, :, 512:1024] = mm
    return m


def _bmask_host(h):
    m = np.zeros((16, 5, 128, 128), dtype=bfdt)
    i = np.arange(128)[:, None]
    j = np.arange(128)[None, :]
    for qt in range(16):
        for et in range(5):
            dd = et * 128 + i - j
            v = (dd >= 0) & (dd <= WIN)
            if h == 0:
                e_abs = (qt + et) * 128 + i
                v = v & np.broadcast_to(e_abs >= PADQ, v.shape)
            m[qt, et] = v.astype(bfdt)
    return m


def _rope_host(h):
    pos = np.maximum(np.arange(KEYS) + (h - 1) * 2048, 0).astype(np.float64)
    inv = 1.0 / 10000.0 ** (np.arange(0, HD, 2) / HD)
    fr = pos[:, None] * inv[None, :]
    fr = fr.reshape(32, 128, 32).transpose(1, 0, 2)
    return (np.ascontiguousarray(np.cos(fr).astype(bfdt)),
            np.ascontiguousarray(np.sin(fr).astype(bfdt)))


def stage_inputs(inputs):
    x = np.ascontiguousarray(inputs["L0_attn_a_x"], dtype=np.float32)
    shared = {
        "wq": np.ascontiguousarray(inputs["L0_attn_a_Wq"][:, _qperm()], dtype=np.float32),
        "wk": np.ascontiguousarray(inputs["L0_attn_a_Wk"], dtype=np.float32),
        "wv": np.ascontiguousarray(inputs["L0_attn_a_Wv"], dtype=np.float32),
        "wo": np.ascontiguousarray(inputs["L0_attn_a_Wo"], dtype=np.float32),
        "bwq": np.ascontiguousarray(inputs["L0_attn_b_wq"], dtype=np.float32),
        "bwk": np.ascontiguousarray(inputs["L0_attn_b_wk"], dtype=np.float32),
        "bwv": np.ascontiguousarray(inputs["L0_attn_b_wv"], dtype=np.float32),
        "bwo": np.ascontiguousarray(inputs["L0_attn_b_wo"], dtype=np.float32),
        "wg": np.ascontiguousarray(inputs["L0_ffn_w_gate"], dtype=np.float32),
        "wu": np.ascontiguousarray(inputs["L0_ffn_w_up"], dtype=np.float32),
        "wd": np.ascontiguousarray(inputs["L0_ffn_w_down"], dtype=np.float32),
        "lq1": np.ascontiguousarray(inputs["L0_attn_a_lam_q1"], dtype=np.float32).reshape(64, 1),
        "lk1": np.ascontiguousarray(inputs["L0_attn_a_lam_k1"], dtype=np.float32).reshape(64, 1),
        "lq2": np.ascontiguousarray(inputs["L0_attn_a_lam_q2"], dtype=np.float32).reshape(64, 1),
        "lk2": np.ascontiguousarray(inputs["L0_attn_a_lam_k2"], dtype=np.float32).reshape(64, 1),
        "g": np.ascontiguousarray(inputs["L0_attn_a_g"], dtype=np.float32).reshape(1, 128),
        "identb": np.eye(128, dtype=bfdt),
        "identf": np.eye(128, dtype=np.float32),
        "dmask": _dmask_host(),
    }
    per_h = {}
    for h in (0, 1):
        rc, rs = _rope_host(h)
        per_h[h] = {"rcos": rc, "rsin": rs, "npad": _npad_host(h),
                    "bmask": _bmask_host(h)}
    maps = []
    for c in range(NCORE):
        bb, h = c // 2, c % 2
        if h == 0:
            xin = np.concatenate([np.zeros((2048, E), np.float32), x[bb, 0:2048]], axis=0)
        else:
            xin = np.ascontiguousarray(x[bb])
        m = {"xin": xin}
        m.update(shared)
        m.update(per_h[h])
        maps.append(m)
    return maps


ENGS = ("sync", "scalar", "vector", "tensor", "gpsimd")


class Bld:
    def __init__(self, nc, ctx):
        self.nc = nc
        self.ops = {e: [] for e in ENGS}
        self.sems = {}
        self.cnt = {}
        self.owner = {}
        self.waited = {e: {} for e in ENGS}
        for name, owner in (("qs", None), ("qg", None), ("spe", "tensor"),
                            ("sdv", "vector"), ("sac", "scalar"), ("spl", "gpsimd")):
            self.sems[name] = ctx.enter_context(nc.semaphore(name))
            self.cnt[name] = 0
            self.owner[name] = owner

    def op(self, eng, fn, wait=(), inc=None, n=1, swait=()):
        wl = []
        for s, v in wait:
            if v <= 0:
                continue
            assert v <= self.cnt[s], f"wait {s}>={v} but only {self.cnt[s]} emitted"
            if self.owner.get(s) == eng:
                continue
            if self.waited[eng].get(s, 0) >= v:
                continue
            self.waited[eng][s] = v
            wl.append((self.sems[s], v))
        for s, v in swait:
            if v <= 0:
                continue
            assert v <= self.cnt[s], f"swait {s}>={v} but only {self.cnt[s]} emitted"
            if self.waited[eng].get(s, 0) >= v:
                continue
            self.waited[eng][s] = v
            wl.append((self.sems[s], v))
        sem = None
        if inc is not None:
            self.cnt[inc] += n
            sem = self.sems[inc]
        self.ops[eng].append((wl, fn, sem, n))
        return self.cnt[inc] if inc else 0

    def dma(self, eng, dst, src, wait=(), **kw):
        q = "qg" if eng == "gpsimd" else "qs"
        return self.op(eng, lambda e: e.dma_start(dst, src, **kw), wait, q, 16)

    def now(self, s):
        return self.cnt[s]

    def all_now(self, *names):
        return [(s, self.cnt[s]) for s in names]

    def emit(self):
        blk_ctx = self.nc.Block()
        blk = blk_ctx.__enter__()
        try:
            for eng, meth in (("gpsimd", blk.gpsimd), ("tensor", blk.tensor),
                              ("vector", blk.vector), ("scalar", blk.scalar),
                              ("sync", blk.sync)):
                ops = self.ops[eng]

                def body(e, ops=ops):
                    for wl, fn, sem, n in ops:
                        for s, v in wl:
                            e.wait_ge(s, v)
                        ins = fn(e)
                        if sem is not None:
                            ins.then_inc(sem, n)
                meth(body)
        finally:
            blk_ctx.__exit__(None, None, None)


def _wsplit(b, eng, dst3, src3, groups, wait=()):
    G = src3.shape[0]
    step = (G + groups - 1) // groups
    r = 0
    for g0 in range(0, G, step):
        g1 = min(G, g0 + step)
        r = b.dma(eng, dst3[:, g0:g1, :], src3[g0:g1].rearrange("g p n -> p g n"),
                  wait=wait if g0 == 0 else ())
    return r


def build_program(stop_after=None, dbg=False):
    nc = bass.Bass("TRN2", target_bir_lowering=False)
    d = {}
    d["xin"] = nc.dram_tensor("xin", [KEYS, E], F32, kind="ExternalInput").ap()
    for n, sh in (("wq", [E, E]), ("wk", [E, 1024]), ("wv", [E, 1024]), ("wo", [E, E]),
                  ("bwq", [E, E]), ("bwk", [E, 512]), ("bwv", [E, 512]), ("bwo", [E, E]),
                  ("wg", [E, FI]), ("wu", [E, FI]), ("wd", [FI, E])):
        d[n] = nc.dram_tensor(n, sh, F32, kind="ExternalInput").ap()
    for n in ("lq1", "lk1", "lq2", "lk2"):
        d[n] = nc.dram_tensor(n, [64, 1], F32, kind="ExternalInput").ap()
    d["g"] = nc.dram_tensor("g", [1, 128], F32, kind="ExternalInput").ap()
    d["identb"] = nc.dram_tensor("identb", [128, 128], BF16, kind="ExternalInput").ap()
    d["identf"] = nc.dram_tensor("identf", [128, 128], F32, kind="ExternalInput").ap()
    d["dmask"] = nc.dram_tensor("dmask", [4, 128, 1024], BF16, kind="ExternalInput").ap()
    d["rcos"] = nc.dram_tensor("rcos", [128, 32, 32], BF16, kind="ExternalInput").ap()
    d["rsin"] = nc.dram_tensor("rsin", [128, 32, 32], BF16, kind="ExternalInput").ap()
    d["npad"] = nc.dram_tensor("npad", [128, 20], F32, kind="ExternalInput").ap()
    d["bmask"] = nc.dram_tensor("bmask", [16, 5, 128, 128], BF16, kind="ExternalInput").ap()
    d["out"] = nc.dram_tensor("out", [OWN, E], F32, kind="ExternalOutput").ap()
    KI = "ExternalOutput" if dbg else "Internal"
    d["xres"] = nc.dram_tensor("xres", [RE, E], F32, kind=KI).ap()
    d["xnt"] = nc.dram_tensor("xnt", [32, 128, E], BF16, kind=KI).ap()
    d["qa"] = nc.dram_tensor("qa", [20, 128, E], BF16, kind=KI).ap()
    d["ka"] = nc.dram_tensor("ka", [32, 128, 1024], BF16, kind=KI).ap()
    d["va"] = nc.dram_tensor("va", [32, 128, 1024], BF16, kind=KI).ap()
    d["anorm"] = nc.dram_tensor("anorm", [16, 128, RE], BF16, kind=KI).ap()
    d["xntb"] = nc.dram_tensor("xntb", [20, 128, E], BF16, kind=KI).ap()
    if dbg:
        d["dbg1"] = nc.dram_tensor("dbg1", [4, 128, E], BF16, kind="ExternalOutput").ap()
        d["dbg2"] = nc.dram_tensor("dbg2", [4, 128, 16], F32, kind="ExternalOutput").ap()

    with ExitStack() as ctx:
        arena = ctx.enter_context(nc.sbuf_tensor("arena", [128, 212832], U8)).ap()
        ps = ctx.enter_context(nc.psum_tensor("ps", [128, 4096], F32)).ap()

        def bank(i, j0=0, n=512):
            return ps[:, i * 512 + j0:i * 512 + j0 + n]

        b = Bld(nc, ctx)

        persist = [212832]

        def palloc(cols, dt, rows=128):
            nb = 4 if dt == F32 else (2 if dt == BF16 else 1)
            persist[0] -= cols * nb
            persist[0] -= persist[0] % 32
            return arena[0:rows, persist[0]:persist[0] + cols * nb].bitcast(dt)

        identb = palloc(128, BF16)
        identf = palloc(128, F32)
        geff = palloc(128, F32)
        nlam = palloc(1, F32)
        npadc = palloc(20, F32)
        onesb = palloc(1, BF16)
        onesf = palloc(1, F32)
        onesrow = palloc(128, F32, rows=1)
        epsb = palloc(1, F32)
        smallf = palloc(16, F32)
        lnst = palloc(16, F32)
        PBASE = persist[0]

        cur = [0]

        def alloc(cols, dt, rows=128):
            nb = 4 if dt == F32 else (2 if dt == BF16 else 1)
            cur[0] = (cur[0] + 31) // 32 * 32
            r = arena[0:rows, cur[0]:cur[0] + cols * nb].bitcast(dt)
            cur[0] += cols * nb
            return r

        def phase_reset(floor=0):
            assert cur[0] <= PBASE, f"arena overflow: {cur[0]} > {PBASE}"
            cur[0] = floor

        b.dma("gpsimd", identb, d["identb"])
        b.dma("gpsimd", identf, d["identf"])
        b.dma("gpsimd", npadc, d["npad"])
        lamv = [alloc(1, F32, rows=64) for _ in range(4)]
        for i, n in enumerate(("lq1", "lk1", "lq2", "lk2")):
            b.dma("gpsimd", lamv[i], d[n])
        grow = alloc(128, F32, rows=1)
        b.dma("gpsimd", grow, d["g"])
        b.op("gpsimd", lambda e: e.memset(onesb, 1.0), inc="spl")
        b.op("gpsimd", lambda e: e.memset(onesf, 1.0), inc="spl")
        b.op("gpsimd", lambda e: e.memset(onesrow, 1.0), inc="spl")
        b.op("gpsimd", lambda e: e.memset(epsb, LNEPS), inc="spl")
        b.op("tensor", lambda e: e.matmul(bank(0, 0, 1)[0:1, :], lamv[0], lamv[1],
                                          start=True, stop=True), wait=b.all_now("qg", "spl"))
        b.op("tensor", lambda e: e.matmul(bank(0, 1, 1)[0:1, :], lamv[2], lamv[3],
                                          start=True, stop=True), inc="spe")
        b.op("scalar", lambda e: e.activation(smallf[0:1, 0:2], bank(0, 0, 2)[0:1, :], AF.Exp),
             wait=b.all_now("spe"), inc="sac")
        h0 = b.op("vector", lambda e: e.tensor_sub(smallf[0:1, 2:3], smallf[0:1, 0:1], smallf[0:1, 1:2]),
                  wait=b.all_now("sac"), inc="sdv")
        b.op("vector", lambda e: e.tensor_scalar(smallf[0:1, 3:4], smallf[0:1, 2:3],
                                                 LAMBDA_INIT, -1.0, ALU.add, ALU.mult),
             swait=[("sdv", h0)], inc="sdv")
        b.op("tensor", lambda e: e.matmul(bank(1, 0, 1), onesrow, smallf[0:1, 3:4],
                                          start=True, stop=True), wait=b.all_now("sdv"))
        b.op("tensor", lambda e: e.matmul(bank(2, 0, 128), onesrow, grow,
                                          start=True, stop=True), inc="spe")
        b.op("vector", lambda e: e.tensor_copy(nlam, bank(1, 0, 1)), wait=b.all_now("spe"))
        b.op("vector", lambda e: e.tensor_scalar_mul(geff, bank(2, 0, 128), 1.0 - LAMBDA_INIT),
             inc="sdv")
        P0_done = b.all_now("sdv", "spl", "qg")

        def emit_ln(src, sqdump, xn_dst, wait_ld=(), war_apply=()):
            b.op("vector", lambda e: e.reduce_sum(lnst[:, 0:1], src, AX), wait=wait_ld)
            f0 = b.op("vector", lambda e: e.scalar_tensor_tensor(
                sqdump, src, 1.0, src, ALU.mult, ALU.mult, accum_out=lnst[:, 1:2]),
                wait=war_apply, inc="sdv")
            b.op("vector", lambda e: e.tensor_scalar_mul(lnst[:, 2:3], lnst[:, 0:1], 1.0 / E),
                 swait=[("sdv", f0)])
            f1 = b.op("vector", lambda e: e.tensor_scalar_mul(lnst[:, 3:4], lnst[:, 1:2], 1.0 / E),
                      inc="sdv")
            f2 = b.op("vector", lambda e: e.tensor_mul(lnst[:, 4:5], lnst[:, 2:3], lnst[:, 2:3]),
                      swait=[("sdv", f1)], inc="sdv")
            b.op("vector", lambda e: e.tensor_sub(lnst[:, 5:6], lnst[:, 3:4], lnst[:, 4:5]),
                 swait=[("sdv", f2)], inc="sdv")
            b.op("scalar", lambda e: e.activation(lnst[:, 6:7], lnst[:, 5:6], AF.Sqrt,
                                                  bias=epsb), wait=b.all_now("sdv"), inc="sac")
            r1 = b.op("vector", lambda e: e.reciprocal(lnst[:, 7:8], lnst[:, 6:7]),
                      wait=b.all_now("sac"), inc="sdv")
            r2 = b.op("vector", lambda e: e.tensor_scalar(
                lnst[:, 8:9], lnst[:, 2:3], lnst[:, 7:8], -1.0, ALU.mult, ALU.mult),
                swait=[("sdv", r1)], inc="sdv")
            return b.op("vector", lambda e: e.tensor_scalar(
                xn_dst, src, lnst[:, 7:8], lnst[:, 8:9], ALU.mult, ALU.add),
                swait=[("sdv", r2)], inc="sdv")

        def emit_rope_tm(buf, nh, cosap, sinap, tmp):
            v3 = buf.rearrange("p (c n) -> p c n", c=nh)
            x1 = v3[:, :, 0:32]
            x2 = v3[:, :, 32:64]
            cb = cosap.unsqueeze(1).broadcast_to([128, nh, 32])
            sb = sinap.unsqueeze(1).broadcast_to([128, nh, 32])
            tv = [tmp[:, i * nh * 32:(i + 1) * nh * 32].rearrange(
                "p (c n) -> p c n", c=nh) for i in range(3)]
            b.op("vector", lambda e: e.tensor_tensor(tv[0], x1, cb, ALU.mult))
            b.op("vector", lambda e: e.tensor_tensor(tv[1], x1, sb, ALU.mult))
            f0 = b.op("vector", lambda e: e.tensor_tensor(tv[2], x2, sb, ALU.mult),
                      inc="sdv")
            b.op("vector", lambda e: e.tensor_add(x1, tv[0], tv[2]), swait=[("sdv", f0)])
            f1 = b.op("vector", lambda e: e.tensor_tensor(tv[0], x2, cb, ALU.mult),
                      inc="sdv")
            return b.op("vector", lambda e: e.tensor_sub(x2, tv[0], tv[1]),
                        swait=[("sdv", f1)], inc="sdv")

        wk_sb = alloc(16 * 1024, BF16)
        wv_sb = alloc(16 * 1024, BF16)
        rct = alloc(32 * 32, BF16)
        rst = alloc(32 * 32, BF16)
        xbf = [alloc(E, BF16) for _ in range(2)]
        xn = [alloc(E, BF16) for _ in range(2)]
        xnt_sb = [alloc(16 * 128, BF16) for _ in range(2)]
        dbgsnap = alloc(64, F32)
        dbgx = alloc(4 * E, BF16)
        ktm = alloc(1024, BF16)
        ktf = alloc(1024, BF16)
        rtmp = alloc(3 * 1024, BF16)
        vst = alloc(1024, BF16)

        _wsplit(b, "gpsimd", wk_sb.rearrange("p (g n) -> p g n", g=16),
                d["wk"].rearrange("(g p) n -> g p n", p=128), 4, wait=P0_done)
        _wsplit(b, "gpsimd", wv_sb.rearrange("p (g n) -> p g n", g=16),
                d["wv"].rearrange("(g p) n -> g p n", p=128), 4)
        b.dma("gpsimd", rct, d["rcos"].rearrange("p a b -> p (a b)"))
        wkv_ready = b.dma("gpsimd", rst, d["rsin"].rearrange("p a b -> p (a b)"))
        rc3 = rct.rearrange("p (a b) -> p a b", a=32)
        rs3 = rst.rearrange("p (a b) -> p a b", a=32)

        for kt in range(32):
            tb = kt % 2
            xt = xnt_sb[tb]
            war = b.all_now("spe") if kt >= 2 else list(P0_done)
            ld = b.dma("gpsimd", xbf[tb], d["xin"][kt * 128:(kt + 1) * 128, :],
                       wait=war)
            emit_ln(xbf[tb], xn[1 - tb], xn[tb], wait_ld=[("qg", ld)],
                    war_apply=b.all_now("spe"))
            xnw = b.now("sdv")
            for c in range(16):
                b.op("tensor", lambda e, c=c, tb=tb: e.matmul(
                    bank(c // 4, (c % 4) * 128, 128), xn[tb][:, c * 128:(c + 1) * 128],
                    identb, start=True, stop=True),
                    wait=([("sdv", xnw), ("qg", wkv_ready)]) if c == 0 else (),
                    inc="spe" if c == 15 else None)
            ev = b.op("vector", lambda e, xt=xt: e.tensor_copy(xt, ps[:, 0:2048]),
                      wait=b.all_now("spe") + [("qs", b.now("qs"))], inc="sdv")
            b.dma("sync", d["xnt"][kt], xt, wait=[("sdv", ev)])
            if dbg and kt < 4:
                b.op("vector", lambda e, kt=kt: e.tensor_copy(dbgsnap[:, kt * 16:(kt + 1) * 16], lnst))
                snp = b.op("vector", lambda e, kt=kt, tb=tb: e.tensor_copy(
                    dbgx[:, kt * E:(kt + 1) * E], xn[tb]), inc="sdv")
                b.dma("sync", d["dbg1"][kt], dbgx[:, kt * E:(kt + 1) * E], wait=[("sdv", snp)])
                b.dma("sync", d["dbg2"][kt], dbgsnap[:, kt * 16:(kt + 1) * 16], wait=[("sdv", snp)])
            for half in range(2):
                for ic in range(16):
                    b.op("tensor", lambda e, half=half, ic=ic, xt=xt: e.matmul(
                        bank(4 + half), xt[:, ic * 128:(ic + 1) * 128],
                        wk_sb[:, ic * 1024 + half * 512:ic * 1024 + (half + 1) * 512],
                        start=(ic == 0), stop=(ic == 15)),
                        wait=[("sdv", ev)] if (half == 0 and ic == 0) else (),
                        inc="spe" if (half == 1 and ic == 15) else None)
            b.op("vector", lambda e: e.tensor_copy(ktm, ps[:, 4 * 512:6 * 512]),
                 wait=b.all_now("spe"), inc="sdv")
            emit_rope_tm(ktm, 16, rc3[:, kt, :], rs3[:, kt, :], rtmp)
            krp = b.now("sdv")
            for c in range(8):
                b.op("tensor", lambda e, c=c: e.matmul(
                    bank(6 + c // 4, (c % 4) * 128, 128), ktm[:, c * 128:(c + 1) * 128],
                    identb, start=True, stop=True),
                    wait=[("sdv", krp)] if c == 0 else (),
                    inc="spe" if c == 7 else None)
            kev = b.op("vector", lambda e: e.tensor_copy(ktf, ps[:, 6 * 512:8 * 512]),
                       wait=b.all_now("spe") + [("qs", b.now("qs"))], inc="sdv")
            b.dma("sync", d["ka"][kt], ktf, wait=[("sdv", kev)])
            for half in range(2):
                for ic in range(16):
                    b.op("tensor", lambda e, half=half, ic=ic, xt=xt: e.matmul(
                        bank(4 + half), xt[:, ic * 128:(ic + 1) * 128],
                        wv_sb[:, ic * 1024 + half * 512:ic * 1024 + (half + 1) * 512],
                        start=(ic == 0), stop=(ic == 15)),
                        wait=[("sdv", b.now("sdv"))] if (half == 0 and ic == 0) else (),
                        inc="spe" if (half == 1 and ic == 15) else None)
            b.op("vector", lambda e: e.tensor_copy(vst, ps[:, 4 * 512:6 * 512]),
                 wait=b.all_now("spe") + [("qs", b.now("qs"))], inc="sdv")
            b.dma("sync", d["va"][kt], vst, wait=b.all_now("sdv"))
        P1a_done = b.all_now("sdv", "spe", "qs", "qg", "sac")
        phase_reset()

        if stop_after == "P1a":
            b.op("sync", lambda e: e.wait_ge(b.sems["qs"], b.now("qs")))
            b.op("sync", lambda e: e.wait_ge(b.sems["qg"], b.now("qg")))
            b.emit()
            return nc, d

        wq_sb = alloc(16 * E, BF16)
        rctq = alloc(20 * 32, BF16)
        rstq = alloc(20 * 32, BF16)
        xntq = [alloc(E, BF16) for _ in range(2)]
        qtm = alloc(E, BF16)
        qtf = alloc(E, BF16)
        rtmpQ = alloc(3 * 2048, BF16)
        _wsplit(b, "gpsimd", wq_sb.rearrange("p (g n) -> p g n", g=16),
                d["wq"].rearrange("(g p) n -> g p n", p=128), 4, wait=P1a_done)
        b.dma("gpsimd", rctq, d["rcos"][:, 12:32, :].rearrange("p a b -> p (a b)"))
        wq_ready = b.dma("gpsimd", rstq, d["rsin"][:, 12:32, :].rearrange("p a b -> p (a b)"))
        rcq3 = rctq.rearrange("p (a b) -> p a b", a=20)
        rsq3 = rstq.rearrange("p (a b) -> p a b", a=20)

        for qt in range(20):
            tb = qt % 2
            ld = b.dma("sync", xntq[tb], d["xnt"][12 + qt],
                       wait=(list(P1a_done) + [("sdv", b.now("sdv"))]) if qt < 2
                       else [("spe", b.now("spe"))])
            for oc4 in range(4):
                for ic in range(16):
                    b.op("tensor", lambda e, oc4=oc4, ic=ic, tb=tb: e.matmul(
                        bank(oc4), xntq[tb][:, ic * 128:(ic + 1) * 128],
                        wq_sb[:, ic * E + oc4 * 512:ic * E + (oc4 + 1) * 512],
                        start=(ic == 0), stop=(ic == 15)),
                        wait=[("qs", ld), ("qg", wq_ready), ("sdv", b.now("sdv"))]
                        if (oc4 == 0 and ic == 0) else (),
                        inc="spe" if (oc4 == 3 and ic == 15) else None)
            b.op("vector", lambda e: e.tensor_copy(qtm, ps[:, 0:2048]),
                 wait=b.all_now("spe"), inc="sdv")
            emit_rope_tm(qtm, 32, rcq3[:, qt, :], rsq3[:, qt, :], rtmpQ)
            qrp = b.now("sdv")
            for c in range(16):
                b.op("tensor", lambda e, c=c: e.matmul(
                    bank(4 + c // 4, (c % 4) * 128, 128), qtm[:, c * 128:(c + 1) * 128],
                    identb, start=True, stop=True),
                    wait=[("sdv", qrp)] if c == 0 else (),
                    inc="spe" if c == 15 else None)
            qev = b.op("vector", lambda e: e.tensor_copy(qtf, ps[:, 4 * 512:8 * 512]),
                       wait=b.all_now("spe") + [("qs", b.now("qs"))], inc="sdv")
            b.dma("sync", d["qa"][qt], qtf, wait=[("sdv", qev)])
        P1b_done = b.all_now("sdv", "spe", "qs", "qg")
        phase_reset()

        if stop_after == "P1b":
            b.op("sync", lambda e: e.wait_ge(b.sems["qs"], b.now("qs")))
            b.op("sync", lambda e: e.wait_ge(b.sems["qg"], b.now("qg")))
            b.emit()
            return nc, d

        ka_sb = alloc(32 * 1024, BF16)
        va_sb = alloc(32 * 1024, BF16)
        q_sb = [alloc(4 * E, BF16) for _ in range(2)]
        p_sb = [alloc(1024, BF16) for _ in range(2)]
        dmaskb = alloc(4 * 1024, BF16)
        l_sb = alloc(1024, F32, rows=1)
        lcol = alloc(16, F32)
        a0f = alloc(512, F32)
        a1f = alloc(512, F32)
        attf = alloc(128, F32)
        sqd = alloc(128, F32)
        anq = alloc(128, BF16)
        an_sb = alloc(512, BF16)
        rmss = alloc(8, F32)

        _wsplit(b, "gpsimd", ka_sb.rearrange("p (g n) -> p g n", g=32), d["ka"], 8,
                wait=P1b_done)
        _wsplit(b, "gpsimd", va_sb.rearrange("p (g n) -> p g n", g=32), d["va"], 8)
        kav_ready = b.dma("gpsimd", dmaskb, d["dmask"].rearrange("g p n -> p g n"))

        pe_qb_mark = {}
        for qb in range(QB_N):
            qsb = q_sb[qb % 2]
            qw = [("qg", kav_ready)]
            if qb >= 2:
                qw.append(("spe", pe_qb_mark[qb - 2]))
            qld = _wsplit(b, "sync", qsb.rearrange("p (g n) -> p g n", g=4),
                          d["qa"][qb * 4:(qb + 1) * 4], 2, wait=qw)
            q3 = qsb.rearrange("p (t c n) -> p t c n", t=4, c=16)
            nkt = KMAX[qb] // 128
            prev_last_exp = [b.now("sac")]
            prev_ev2 = [b.now("sdv")]
            for dh in range(NDH):
                kc, base = dh // 2, 64 * (dh % 2)
                qc = 2 * (dh // 2)
                vcol = (dh // 2) * 128
                exp_mark = {}
                pe_pmark = {}
                mask_dv = {}

                def emit_lav(kt):
                    P = kt % 2
                    pb = p_sb[P]
                    wav = ([("sdv", mask_dv[kt])] if mask_dv.get(kt)
                           else [("sac", exp_mark[kt])])
                    if kt == 0:
                        wav.append(("sdv", prev_ev2[0]))
                    b.op("tensor", lambda e, pb=pb, kt=kt: e.matmul(
                        bank(6, 0, 512)[0:1, :], onesb, pb[:, 0:512],
                        start=(kt == 0), stop=(kt == nkt - 1)), wait=wav)
                    b.op("tensor", lambda e, pb=pb, kt=kt: e.matmul(
                        bank(7, 0, 512)[0:1, :], onesb, pb[:, 512:1024],
                        start=(kt == 0), stop=(kt == nkt - 1)))
                    b.op("tensor", lambda e, pb=pb, kt=kt, vcol=vcol: e.matmul(
                        bank(4), va_sb[:, kt * 1024 + vcol:kt * 1024 + vcol + 128],
                        pb[:, 0:512], start=(kt == 0), stop=(kt == nkt - 1)))
                    pe_pmark[kt] = b.op("tensor", lambda e, pb=pb, kt=kt, vcol=vcol: e.matmul(
                        bank(5), va_sb[:, kt * 1024 + vcol:kt * 1024 + vcol + 128],
                        pb[:, 512:1024], start=(kt == 0), stop=(kt == nkt - 1)), inc="spe")

                for kt in range(nkt):
                    P = kt % 2
                    sb0 = P * 2
                    pb = p_sb[P]
                    w = [("qs", qld)]
                    if kt >= 2:
                        w.append(("sac", exp_mark[kt - 2]))
                    if kt <= 1:
                        w.append(("sac", prev_last_exp[0]))
                    b.op("tensor", lambda e, kc=kc, base=base, kt=kt, sb0=sb0, qc=qc, q3=q3: e.matmul(
                        bank(sb0),
                        ka_sb[base:base + 64, kt * 1024 + kc * 128:kt * 1024 + (kc + 1) * 128],
                        q3[base:base + 64, :, qc, :],
                        start=True, stop=True), wait=w)
                    b.op("tensor", lambda e, kc=kc, base=base, kt=kt, sb0=sb0, qc=qc, q3=q3: e.matmul(
                        bank(sb0 + 1),
                        ka_sb[base:base + 64, kt * 1024 + kc * 128:kt * 1024 + (kc + 1) * 128],
                        q3[base:base + 64, :, qc + 1, :],
                        start=True, stop=True), inc="spe")
                    smm = b.now("spe")
                    wexp = [("spe", smm)]
                    if kt >= 2:
                        wexp.append(("spe", pe_pmark[kt - 2]))
                    exp_mark[kt] = b.op("scalar", lambda e, sb0=sb0, pb=pb: e.activation(
                        pb, ps[:, sb0 * 512:(sb0 + 2) * 512], AF.Exp, scale=SCALE_A),
                        wait=wexp, inc="sac")
                    o = kt - (qb * 4 + 12)
                    if 0 <= o <= 3:
                        mask_dv[kt] = b.op("vector", lambda e, pb=pb, o=o: e.tensor_mul(
                            pb, pb, dmaskb[:, o * 1024:(o + 1) * 1024]),
                            wait=[("sac", exp_mark[kt])], inc="sdv")
                    if kt >= 1:
                        emit_lav(kt - 1)
                emit_lav(nkt - 1)
                kdone = b.now("spe")
                b.op("vector", lambda e: e.tensor_copy(l_sb, ps[0:1, 6 * 512:8 * 512]),
                     wait=[("spe", kdone)])
                b.op("vector", lambda e: e.tensor_copy(a0f, bank(4)))
                aev = b.op("vector", lambda e: e.tensor_copy(a1f, bank(5)), inc="sdv")
                for s in range(2):
                    for q4 in range(4):
                        b.op("tensor", lambda e, s=s, q4=q4: e.matmul(
                            bank(6, s * 4 + q4, 1),
                            l_sb[0:1, s * 512 + q4 * 128:s * 512 + (q4 + 1) * 128],
                            onesf[0:1, 0:1], start=True, stop=True),
                            wait=[("sdv", aev)] if (s == 0 and q4 == 0) else (),
                            inc="spe" if (s == 1 and q4 == 3) else None)
                lmm = b.now("spe")
                g0 = b.op("vector", lambda e: e.tensor_copy(lcol[:, 0:8], bank(6, 0, 8)),
                          wait=[("spe", lmm)], inc="sdv")
                b.op("vector", lambda e, qb=qb: e.tensor_sub(
                    lcol[:, 0:4], lcol[:, 0:4], npadc[:, qb * 4:(qb + 1) * 4]),
                    swait=[("sdv", g0)])
                g1 = b.op("vector", lambda e, qb=qb: e.tensor_sub(
                    lcol[:, 4:8], lcol[:, 4:8], npadc[:, qb * 4:(qb + 1) * 4]), inc="sdv")
                g2 = b.op("vector", lambda e: e.reciprocal(lcol[:, 8:16], lcol[:, 0:8]),
                          swait=[("sdv", g1)], inc="sdv")
                b.op("vector", lambda e: e.tensor_tensor(
                    lcol[:, 4:8], lcol[:, 12:16], nlam.broadcast_to([128, 4]), ALU.mult),
                    swait=[("sdv", g2)], inc="sdv")
                linv = b.now("sdv")
                for q4 in range(4):
                    b.op("tensor", lambda e, q4=q4: e.transpose(
                        bank(4, q4 * 128, 128), a0f[:, q4 * 128:(q4 + 1) * 128], identf),
                        wait=[("sdv", aev)] if q4 == 0 else ())
                    b.op("tensor", lambda e, q4=q4: e.transpose(
                        bank(5, q4 * 128, 128), a1f[:, q4 * 128:(q4 + 1) * 128], identf),
                        inc="spe" if q4 == 3 else None)
                tdone = b.now("spe")
                sqm = 0
                btm = 0
                for q4 in range(4):
                    ww = [("spe", tdone), ("sdv", linv)]
                    if sqm:
                        ww.append(("sac", sqm))
                    b.op("vector", lambda e, q4=q4: e.tensor_scalar(
                        attf, bank(4, q4 * 128, 128), lcol[:, 8 + q4:9 + q4], None, ALU.mult),
                        wait=ww, swait=[("sdv", linv)])
                    b.op("vector", lambda e, q4=q4: e.scalar_tensor_tensor(
                        attf, bank(5, q4 * 128, 128), lcol[:, 4 + q4:5 + q4], attf,
                        ALU.mult, ALU.add), inc="sdv")
                    sqm = b.op("scalar", lambda e: e.activation(sqd, attf, AF.Square,
                                                                accum_out=rmss[:, 0:1]),
                               wait=b.all_now("sdv"), inc="sac")
                    b.op("vector", lambda e: e.tensor_scalar(
                        rmss[:, 1:2], rmss[:, 0:1], 1.0 / VD, RMSEPS, ALU.mult, ALU.add),
                        wait=[("sac", sqm)], inc="sdv")
                    b.op("scalar", lambda e: e.activation(rmss[:, 2:3], rmss[:, 1:2], AF.Sqrt),
                         wait=b.all_now("sdv"), inc="sac")
                    r3 = b.op("vector", lambda e: e.reciprocal(rmss[:, 3:4], rmss[:, 2:3]),
                              wait=b.all_now("sac"), inc="sdv")
                    wanq = [("spe", btm)] if btm else []
                    b.op("vector", lambda e: e.scalar_tensor_tensor(
                        anq, attf, rmss[:, 3:4], geff, ALU.mult, ALU.mult),
                        wait=wanq, swait=[("sdv", r3)], inc="sdv")
                    btm = b.op("tensor", lambda e, q4=q4: e.matmul(
                        bank(7, q4 * 128, 128), anq, identb, start=True, stop=True),
                        wait=b.all_now("sdv"), inc="spe")
                ev2 = b.op("vector", lambda e: e.tensor_copy(an_sb, bank(7)),
                           wait=[("spe", btm), ("qs", b.now("qs"))], inc="sdv")
                b.dma("sync", d["anorm"][dh][:, qb * 512:(qb + 1) * 512], an_sb,
                      wait=[("sdv", ev2)])
                prev_last_exp[0] = exp_mark[nkt - 1]
                prev_ev2[0] = ev2
            pe_qb_mark[qb] = b.now("spe")
        P2_done = b.all_now("sdv", "spe", "qs", "qg", "sac")
        phase_reset()

        if stop_after == "P2":
            b.op("sync", lambda e: e.wait_ge(b.sems["qs"], b.now("qs")))
            b.op("sync", lambda e: e.wait_ge(b.sems["qg"], b.now("qg")))
            b.emit()
            return nc, d

        wo_sb = alloc(16 * E, BF16)
        an16 = [alloc(16 * 512, BF16) for _ in range(2)]
        xint = [alloc(E, F32) for _ in range(2)]
        xrout = [alloc(E, F32) for _ in range(2)]
        _wsplit(b, "gpsimd", wo_sb.rearrange("p (g n) -> p g n", g=16),
                d["wo"].rearrange("(g p) n -> g p n", p=128), 4, wait=P2_done)
        wo_ready = b.now("qg")
        qb_pe = {}
        for qb in range(QB_N):
            sb = qb % 2
            aw = list(P2_done)
            if qb >= 2:
                aw.append(("spe", qb_pe[qb - 2]))
            anl = _wsplit(b, "sync", an16[sb].rearrange("p (g n) -> p g n", g=16),
                          d["anorm"][:, :, qb * 512:(qb + 1) * 512], 4, wait=aw)
            for q4 in range(4):
                tb = q4 % 2
                qt = qb * 4 + q4
                xl = b.dma("sync", xint[tb], d["xin"][(12 + qt) * 128:(13 + qt) * 128, :],
                           wait=b.all_now("sdv"))
                for do in range(4):
                    for dh in range(16):
                        b.op("tensor", lambda e, do=do, dh=dh, q4=q4, sb=sb: e.matmul(
                            bank(do), an16[sb][:, dh * 512 + q4 * 128:dh * 512 + (q4 + 1) * 128],
                            wo_sb[:, dh * E + do * 512:dh * E + (do + 1) * 512],
                            start=(dh == 0), stop=(dh == 15)),
                            wait=[("qs", anl), ("qg", wo_ready), ("sdv", b.now("sdv"))]
                            if (do == 0 and dh == 0) else (),
                            inc="spe" if (do == 3 and dh == 15) else None)
                mm = b.now("spe")
                ra = b.op("vector", lambda e, tb=tb: e.tensor_add(
                    xrout[tb], xint[tb], ps[:, 0:2048]),
                    wait=[("spe", mm), ("qs", xl)], inc="sdv")
                b.dma("sync", d["xres"][qt * 128:(qt + 1) * 128, :], xrout[tb],
                      wait=[("sdv", ra)])
            qb_pe[qb] = b.now("spe")
        P2b_done = b.all_now("sdv", "spe", "qs", "qg")
        phase_reset()

        if stop_after == "P2b":
            b.op("sync", lambda e: e.wait_ge(b.sems["qs"], b.now("qs")))
            b.op("sync", lambda e: e.wait_ge(b.sems["qg"], b.now("qg")))
            b.emit()
            return nc, d

        def emit_ffn_group(g0, gn, to_xres, wait0):
            nt = gn // 128
            xnt_g = alloc(16 * gn, BF16)
            wgb = [alloc(16 * 512, BF16) for _ in range(2)]
            wub = [alloc(16 * 512, BF16) for _ in range(2)]
            wdb = alloc(4 * 2048, BF16)
            acc = alloc(nt * 2048, F32)
            xint2 = [alloc(E, F32) for _ in range(2)]
            xn2 = [alloc(E, BF16) for _ in range(2)]
            sg2 = [alloc(512, BF16) for _ in range(2)]
            hb2 = [alloc(512, BF16) for _ in range(2)]
            hsp2 = [alloc(512, BF16) for _ in range(2)]
            hspT2 = [alloc(512, BF16) for _ in range(2)]
            sp3 = [alloc(512, F32) for _ in range(3)]

            ln_ev = []
            for t in range(nt):
                tb = t % 2
                war = [("sdv", ln_ev[-2])] if len(ln_ev) >= 2 else list(wait0)
                ld = b.dma("sync", xint2[tb], d["xres"][g0 + t * 128:g0 + (t + 1) * 128, :],
                           wait=war)
                emit_ln(xint2[tb], xn2[1 - tb], xn2[tb], wait_ld=[("qs", ld)],
                        war_apply=b.all_now("spe"))
                xnw = b.now("sdv")
                b0 = (t % 2) * 4
                for c in range(16):
                    bk = b0 + c // 4
                    b.op("tensor", lambda e, c=c, bk=bk, tb=tb: e.matmul(
                        bank(bk, (c % 4) * 128, 128), xn2[tb][:, c * 128:(c + 1) * 128],
                        identb, start=True, stop=True),
                        wait=([("sdv", xnw)] + ([("sdv", ln_ev[-1])] if ln_ev else []))
                        if c == 0 else (), inc="spe" if c == 15 else None)
                ev = b.op("vector", lambda e, t=t, b0=b0: e.tensor_copy(
                    xnt_g[:, t * 2048:(t + 1) * 2048], ps[:, b0 * 512:(b0 + 4) * 512]),
                    wait=b.all_now("spe"), inc="sdv")
                ln_ev.append(ev)
            ln_done = b.now("sdv")
            xg3 = xnt_g.rearrange("p (t c n) -> p t c n", t=nt, c=16)

            silu_mark = {}
            evT_mark = {}
            acc_mark = {}
            hm_mark = {}
            trm_mark = {}
            dmm_mark = {}
            pe_chunk = {}
            for ic in range(NIC):
                buf = ic % 2
                wwait = [("sdv", ln_done)]
                if ic >= 2:
                    wwait.append(("spe", pe_chunk[ic - 2]))
                _wsplit(b, "gpsimd", wgb[buf].rearrange("p (g n) -> p g n", g=16),
                        d["wg"][:, ic * 512:(ic + 1) * 512].rearrange("(g p) n -> g p n", p=128),
                        4, wait=wwait)
                wgu = _wsplit(b, "gpsimd", wub[buf].rearrange("p (g n) -> p g n", g=16),
                        d["wu"][:, ic * 512:(ic + 1) * 512].rearrange("(g p) n -> g p n", p=128), 4)
                wdv3 = wdb.rearrange("p (g n) -> p g n", g=4)
                wdw = [("spe", pe_chunk[ic - 1])] if ic >= 1 else ()
                wld0 = _wsplit(b, "gpsimd", wdv3[:, :, 0:1024],
                               d["wd"][ic * 512:(ic + 1) * 512, 0:1024].rearrange("(g p) n -> g p n", p=128),
                               2, wait=wdw)
                wld = _wsplit(b, "gpsimd", wdv3[:, :, 1024:2048],
                              d["wd"][ic * 512:(ic + 1) * 512, 1024:2048].rearrange("(g p) n -> g p n", p=128), 2)
                def emit_front(t):
                    tb = t % 2
                    pA = (t % 2) * 2
                    key = (ic, t)
                    pkey = (ic, t - 2) if t >= 2 else ((ic - 1, nt - 2 + t) if ic >= 1 else None)
                    w = [("qg", wgu)]
                    if pkey is not None and pkey in evT_mark:
                        w.append(("sac", evT_mark[pkey]))
                        w.append(("sdv", hm_mark[pkey]))
                    for rc in range(16):
                        b.op("tensor", lambda e, rc=rc, t=t, pA=pA, buf=buf: e.matmul(
                            bank(pA), xg3[:, t, rc, :], wgb[buf][:, rc * 512:(rc + 1) * 512],
                            start=(rc == 0), stop=(rc == 15)), wait=w if rc == 0 else ())
                        b.op("tensor", lambda e, rc=rc, t=t, pA=pA, buf=buf: e.matmul(
                            bank(pA + 1), xg3[:, t, rc, :], wub[buf][:, rc * 512:(rc + 1) * 512],
                            start=(rc == 0), stop=(rc == 15)),
                            inc="spe" if rc == 15 else None)
                    gmm = b.now("spe")
                    wsilu = [("spe", gmm)]
                    if pkey is not None and pkey in hm_mark:
                        wsilu.append(("sdv", hm_mark[pkey]))
                    silu_mark[key] = b.op("scalar", lambda e, tb=tb, pA=pA: e.activation(
                        sg2[tb], bank(pA), AF.Silu), wait=wsilu, inc="sac")
                    wh = [("sac", silu_mark[key])]
                    fh = b.op("vector", lambda e, tb=tb, pA=pA: e.tensor_tensor(
                        hb2[tb], sg2[tb], bank(pA + 1), ALU.mult),
                        wait=wh, inc="sdv")
                    h4 = hb2[tb].rearrange("p (n f) -> p n f", f=4)
                    hs4 = hsp2[tb].rearrange("p (n f) -> p n f", f=4)
                    a = lambda i: sp3[0][:, i * 128:(i + 1) * 128]
                    mm_ = lambda i: sp3[1][:, i * 128:(i + 1) * 128]
                    z = lambda i: sp3[2][:, i * 128:(i + 1) * 128]
                    wp = []
                    if pkey is not None and pkey in trm_mark:
                        wp.append(("spe", trm_mark[pkey]))
                    for i in range(4):
                        b.op("vector", lambda e, i=i, h4=h4, a=a: e.tensor_mul(
                            a(i), h4[:, :, i], h4[:, :, i]),
                            wait=wp if i == 0 else (), swait=[("sdv", fh)] if i == 0 else (),
                            inc="sdv" if i == 3 else None)
                    fa = b.now("sdv")
                    b.op("vector", lambda e, a=a, m=mm_: e.tensor_tensor(m(0), a(0), a(1), ALU.min),
                         swait=[("sdv", fa)])
                    b.op("vector", lambda e, a=a, m=mm_: e.tensor_tensor(m(1), a(2), a(3), ALU.min))
                    b.op("vector", lambda e, a=a, m=mm_: e.tensor_tensor(m(2), a(0), a(1), ALU.max))
                    fm = b.op("vector", lambda e, a=a, m=mm_: e.tensor_tensor(m(3), a(2), a(3), ALU.max),
                              inc="sdv")
                    b.op("vector", lambda e, m=mm_, z=z: e.tensor_tensor(z(0), m(0), m(1), ALU.max),
                         swait=[("sdv", fm)])
                    fz = b.op("vector", lambda e, m=mm_, z=z: e.tensor_tensor(z(1), m(2), m(3), ALU.min),
                              inc="sdv")
                    fz2 = b.op("vector", lambda e, z=z: e.tensor_tensor(z(2), z(0), z(1), ALU.max),
                               swait=[("sdv", fz)], inc="sdv")
                    km = sp3[1].rearrange("p (n f) -> p n f", f=4)
                    for i in range(4):
                        b.op("vector", lambda e, i=i, a=a, km=km, z=z: e.tensor_tensor(
                            km[:, :, i], a(i), z(2), ALU.is_ge),
                            swait=[("sdv", fz2)] if i == 0 else (),
                            inc="sdv" if i == 3 else None)
                    fk = b.now("sdv")
                    hm_mark[key] = b.op("vector", lambda e, tb=tb: e.tensor_tensor(
                        hsp2[tb], hb2[tb], sp3[1], ALU.mult),
                        swait=[("sdv", fk)], inc="sdv")

                def emit_back(t):
                    tb = t % 2
                    pA = (t % 2) * 2
                    key = (ic, t)
                    pkey = (ic, t - 2) if t >= 2 else ((ic - 1, nt - 2 + t) if ic >= 1 else None)
                    for s_ in range(4):
                        b.op("tensor", lambda e, s=s_, tb=tb, pA=pA: e.matmul(
                            bank(pA, s * 128, 128), hsp2[tb][:, s * 128:(s + 1) * 128],
                            identb, start=True, stop=True),
                            wait=[("sdv", hm_mark[key])] if s_ == 0 else (),
                            inc="spe" if s_ == 3 else None)
                    trm = b.now("spe")
                    trm_mark[key] = trm
                    wev = [("spe", trm)]
                    dkey = (ic, t - 2) if t >= 2 else ((ic - 1, nt - 2 + t) if ic >= 1 else None)
                    if dkey is not None and dkey in dmm_mark:
                        wev.append(("spe", dmm_mark[dkey]))
                    evT_mark[key] = b.op("scalar", lambda e, tb=tb, pA=pA: e.copy(
                        hspT2[tb], bank(pA)), wait=wev, inc="sac")

                def emit_down(t):
                    tb = t % 2
                    key = (ic, t)
                    wdown = [("sac", evT_mark[key]), ("qg", wld0)]
                    akey = (ic, t - 1) if t >= 1 else ((ic - 1, nt - 1) if ic >= 1 else None)
                    if akey is not None and akey in acc_mark:
                        wdown.append(("sdv", acc_mark[akey]))
                    wdown.append(("qg", wld))
                    for s_ in range(4):
                        for o in range(4):
                            wv = wdown if (s_ == 0 and o == 0) else ()
                            b.op("tensor", lambda e, s=s_, o=o, tb=tb: e.matmul(
                                bank(4 + o), hspT2[tb][:, s * 128:(s + 1) * 128],
                                wdb[:, s * 2048 + o * 512:s * 2048 + (o + 1) * 512],
                                start=(s == 0), stop=(s == 3)),
                                wait=wv,
                                inc="spe" if (s_ == 3 and o == 3) else None)
                    dmm_mark[key] = b.now("spe")

                def emit_acc(t):
                    key = (ic, t)
                    r = 0
                    for o in range(4):
                        r = b.op("vector", lambda e, o=o, t=t, ic=ic: (
                            e.tensor_copy(acc[:, t * 2048 + o * 512:t * 2048 + (o + 1) * 512],
                                          bank(4 + o))
                            if ic == 0 else
                            e.tensor_add(acc[:, t * 2048 + o * 512:t * 2048 + (o + 1) * 512],
                                         acc[:, t * 2048 + o * 512:t * 2048 + (o + 1) * 512],
                                         bank(4 + o))),
                            wait=[("spe", dmm_mark[key])] if o == 0 else (),
                            inc="sdv")
                    acc_mark[key] = r

                for t in range(nt):
                    emit_front(t)
                    if t >= 1:
                        emit_back(t - 1)
                    if t >= 2:
                        emit_down(t - 2)
                        emit_acc(t - 2)
                emit_back(nt - 1)
                for t in (nt - 2, nt - 1):
                    emit_down(t)
                    emit_acc(t)
                pe_chunk[ic] = b.now("spe")
            for t in range(nt):
                tb = t % 2
                xl = b.dma("sync", xint2[tb], d["xres"][g0 + t * 128:g0 + (t + 1) * 128, :],
                           wait=[("sdv", acc_mark[(NIC - 1, min(t, nt - 1))]),
                                 ("spe", b.now("spe"))])
                ra = b.op("vector", lambda e, tb=tb, t=t: e.tensor_add(
                    xint2[tb], xint2[tb], acc[:, t * 2048:(t + 1) * 2048]),
                    wait=[("qs", xl)], inc="sdv")
                if to_xres:
                    b.dma("sync", d["xres"][g0 + t * 128:g0 + (t + 1) * 128, :], xint2[tb],
                          wait=[("sdv", ra)])
                else:
                    b.dma("sync", d["out"][g0 - PADQ + t * 128:g0 - PADQ + (t + 1) * 128, :],
                          xint2[tb], wait=[("sdv", ra)])
            r = b.all_now("sdv", "spe", "qs", "qg", "sac")
            phase_reset()
            return r

        def emit_ffn(row0, nrows, to_xres, wait0):
            done = wait0
            g0 = row0
            while g0 < row0 + nrows:
                gn = min(896, row0 + nrows - g0)
                done = emit_ffn_group(g0, gn, to_xres, done)
                g0 += gn
            return done

        f1_done = emit_ffn(0, RE, True, P2b_done)
        if stop_after == "P3":
            b.op("sync", lambda e: e.wait_ge(b.sems["qs"], b.now("qs")))
            b.op("sync", lambda e: e.wait_ge(b.sems["qg"], b.now("qg")))
            b.emit()
            return nc, d

        kb_sb = alloc(4 * RE, BF16)
        vb_sb = alloc(20 * 512, BF16)
        qfm = alloc(16 * OWN, BF16)
        P4_FLOOR = cur[0]

        bwk_sb = alloc(16 * 512, BF16)
        bwv_sb = alloc(16 * 512, BF16)
        xint3 = [alloc(E, F32) for _ in range(2)]
        xn3 = [alloc(E, BF16) for _ in range(2)]
        xnt3_sb = [alloc(16 * 512, BF16) for _ in range(2)]
        _wsplit(b, "gpsimd", bwk_sb.rearrange("p (g n) -> p g n", g=16),
                d["bwk"].rearrange("(g p) n -> g p n", p=128), 4, wait=f1_done)
        bw_ready = _wsplit(b, "gpsimd", bwv_sb.rearrange("p (g n) -> p g n", g=16),
                           d["bwv"].rearrange("(g p) n -> g p n", p=128), 4)
        ev3 = []
        for kb5 in range(5):
            xt = xnt3_sb[kb5 % 2]
            for t4 in range(4):
                rt = kb5 * 4 + t4
                tb = rt % 2
                war = [("sdv", ev3[-2])] if len(ev3) >= 2 else list(f1_done)
                ld = b.dma("sync", xint3[tb], d["xres"][rt * 128:(rt + 1) * 128, :],
                           wait=war)
                emit_ln(xint3[tb], xn3[1 - tb], xn3[tb], wait_ld=[("qs", ld)],
                        war_apply=b.all_now("spe"))
                xnw = b.now("sdv")
                b0 = (t4 % 2) * 4
                for c in range(16):
                    bk = b0 + c // 4
                    b.op("tensor", lambda e, c=c, bk=bk, tb=tb: e.matmul(
                        bank(bk, (c % 4) * 128, 128), xn3[tb][:, c * 128:(c + 1) * 128],
                        identb, start=True, stop=True),
                        wait=([("sdv", xnw), ("qg", bw_ready)]
                              + ([("sdv", ev3[-1])] if ev3 else [])) if c == 0 else (),
                        inc="spe" if c == 15 else None)
                ev = b.op("vector", lambda e, t4=t4, b0=b0, xt=xt: e.tensor_copy(
                    xt[:, t4 * 2048:(t4 + 1) * 2048], ps[:, b0 * 512:(b0 + 4) * 512]),
                    wait=b.all_now("spe") + [("qs", b.now("qs"))], inc="sdv")
                ev3.append(ev)
                b.dma("sync", d["xntb"][rt], xt[:, t4 * 2048:(t4 + 1) * 2048],
                      wait=[("sdv", ev)])
            x3 = xt.rearrange("p (t c n) -> p t c n", t=4, c=16)
            kw = b.now("sdv")
            for kc in range(4):
                for ic in range(16):
                    b.op("tensor", lambda e, kc=kc, ic=ic, x3=x3: e.matmul(
                        bank(kc), bwk_sb[:, ic * 512 + kc * 128:ic * 512 + (kc + 1) * 128],
                        x3[:, :, ic, :], start=(ic == 0), stop=(ic == 15)),
                        wait=[("sdv", kw)] if (kc == 0 and ic == 0) else (),
                        inc="spe" if (kc == 3 and ic == 15) else None)
            kmm = b.now("spe")
            for kc in range(4):
                b.op("vector", lambda e, kc=kc, kb5=kb5: e.tensor_copy(
                    kb_sb[:, kc * RE + kb5 * 512:kc * RE + (kb5 + 1) * 512], bank(kc)),
                    wait=[("spe", kmm)] if kc == 0 else (), inc="sdv" if kc == 3 else None)
            for t4 in range(4):
                rt = kb5 * 4 + t4
                pbk = 4 + (t4 % 2)
                for ic in range(16):
                    b.op("tensor", lambda e, t4=t4, ic=ic, pbk=pbk, xt=xt: e.matmul(
                        bank(pbk), xt[:, (t4 * 16 + ic) * 128:(t4 * 16 + ic + 1) * 128],
                        bwv_sb[:, ic * 512:(ic + 1) * 512],
                        start=(ic == 0), stop=(ic == 15)),
                        wait=[("sdv", b.now("sdv"))] if ic == 0 else (),
                        inc="spe" if ic == 15 else None)
                b.op("vector", lambda e, rt=rt, pbk=pbk: e.tensor_copy(
                    vb_sb[:, rt * 512:(rt + 1) * 512], bank(pbk)),
                    wait=b.all_now("spe"), inc="sdv")
        P4a_done = b.all_now("sdv", "spe", "qs", "qg", "sac")
        phase_reset(P4_FLOOR)

        bwq_sb = alloc(16 * E, BF16)
        xntq3 = alloc(4 * E, BF16)
        _wsplit(b, "gpsimd", bwq_sb.rearrange("p (g n) -> p g n", g=16),
                d["bwq"].rearrange("(g p) n -> g p n", p=128), 4, wait=P4a_done)
        bq_ready = b.now("qg")
        for qb4 in range(4):
            lds = 0
            for t4 in range(4):
                rt = 4 + qb4 * 4 + t4
                lds = b.dma("sync", xntq3[:, t4 * E:(t4 + 1) * E], d["xntb"][rt],
                            wait=list(P4a_done) + [("spe", b.now("spe"))])
            xq3b = xntq3.rearrange("p (t c n) -> p t c n", t=4, c=16)
            for half in range(2):
                for oc in range(8):
                    occ = half * 8 + oc
                    for ic in range(16):
                        b.op("tensor", lambda e, occ=occ, oc=oc, ic=ic, xq3b=xq3b: e.matmul(
                            bank(oc), bwq_sb[:, ic * E + occ * 128:ic * E + occ * 128 + 128],
                            xq3b[:, :, ic, :], start=(ic == 0), stop=(ic == 15)),
                            wait=[("qs", lds), ("qg", bq_ready), ("sdv", b.now("sdv"))]
                            if (oc == 0 and ic == 0) else (),
                            inc="spe" if (oc == 7 and ic == 15) else None)
                qmm = b.now("spe")
                for oc in range(8):
                    occ = half * 8 + oc
                    b.op("vector", lambda e, occ=occ, oc=oc, qb4=qb4: e.tensor_copy(
                        qfm[:, occ * OWN + qb4 * 512:occ * OWN + (qb4 + 1) * 512], bank(oc)),
                        wait=[("spe", qmm)] if oc == 0 else (),
                        inc="sdv" if oc == 7 else None)
        P4b_done = b.all_now("sdv", "spe", "qs", "qg")
        phase_reset(P4_FLOOR)

        bwo_sb = alloc(16 * E, BF16)
        bm_sb = [alloc(5 * 128, BF16) for _ in range(2)]
        pb_sb = [alloc(640, BF16) for _ in range(2)]
        abf = alloc(16 * 128, BF16)
        lrow = alloc(16 * 128, F32, rows=1)
        linv = alloc(16 * 128, F32, rows=1)
        xint4_t = alloc(E, F32)
        xint4 = [xint4_t, xint4_t]
        _wsplit(b, "gpsimd", bwo_sb.rearrange("p (g n) -> p g n", g=16),
                d["bwo"].rearrange("(g p) n -> g p n", p=128), 4, wait=P4b_done)
        bo_ready = b.now("qg")
        for qt in range(16):
            mb = qt % 2
            bml = b.dma("sync", bm_sb[mb].rearrange("p (g n) -> p g n", g=5),
                        d["bmask"][qt].rearrange("g p n -> p g n"),
                        wait=b.all_now("sdv") + [("spe", b.now("spe"))])
            xl = b.dma("sync", xint4[mb], d["xres"][(4 + qt) * 128:(5 + qt) * 128, :],
                       wait=b.all_now("sdv") + [("qs", b.now("qs"))])
            abf_dv = 0
            for hh in range(HB):
                kvh = hh // 4
                pb = pb_sb[hh % 2]
                for et in range(5):
                    bk, off = (0, et * 128) if et < 4 else (1, 0)
                    w = []
                    if et == 0:
                        w = [("qs", bml), ("qg", bo_ready), ("sac", b.now("sac")),
                             ("sdv", b.now("sdv"))]
                    b.op("tensor", lambda e, kvh=kvh, qt=qt, et=et, bk=bk, off=off, hh=hh: e.matmul(
                        bank(bk, off, 128),
                        kb_sb[:, kvh * RE + (qt + et) * 128:kvh * RE + (qt + et + 1) * 128],
                        qfm[:, hh * OWN + qt * 128:hh * OWN + (qt + 1) * 128],
                        start=True, stop=True), wait=w, inc="spe" if et == 4 else None)
                smm = b.now("spe")
                em = b.op("scalar", lambda e, pb=pb: e.activation(
                    pb, ps[:, 0:640], AF.Exp, scale=SCALE_B),
                    wait=[("spe", smm), ("spe", b.now("spe"))], inc="sac")
                mk = b.op("vector", lambda e, pb=pb, mb=mb: e.tensor_mul(pb, pb, bm_sb[mb]),
                          wait=[("sac", em)], inc="sdv")
                for et in range(5):
                    b.op("tensor", lambda e, pb=pb, et=et: e.matmul(
                        bank(2, 0, 128)[0:1, :], onesb, pb[:, et * 128:(et + 1) * 128],
                        start=(et == 0), stop=(et == 4)),
                        wait=[("sdv", mk)] if et == 0 else ())
                for et in range(5):
                    b.op("tensor", lambda e, pb=pb, et=et, qt=qt, kvh=kvh: e.matmul(
                        bank(3, 0, 128), vb_sb[:, (qt + et) * 512 + kvh * 128:(qt + et) * 512 + (kvh + 1) * 128],
                        pb[:, et * 128:(et + 1) * 128], start=(et == 0), stop=(et == 4)),
                        inc="spe" if et == 4 else None)
                avm = b.now("spe")
                b.op("vector", lambda e, hh=hh: e.tensor_copy(
                    abf[:, hh * 128:(hh + 1) * 128], bank(3, 0, 128)), wait=[("spe", avm)])
                abf_dv = b.op("vector", lambda e, hh=hh: e.tensor_copy(
                    lrow[0:1, hh * 128:(hh + 1) * 128], bank(2, 0, 128)[0:1, :]),
                    inc="sdv")
            b.op("vector", lambda e: e.reciprocal(linv, lrow), swait=[("sdv", abf_dv)],
                 inc="sdv")
            rec = b.now("sdv")
            for hh in range(16):
                bk = 4 + hh // 4
                b.op("tensor", lambda e, hh=hh, bk=bk: e.matmul(
                    bank(bk, (hh % 4) * 128, 128), onesrow,
                    linv[0:1, hh * 128:(hh + 1) * 128], start=True, stop=True),
                    wait=[("sdv", rec)] if hh == 0 else (),
                    inc="spe" if hh == 15 else None)
            bcm = b.now("spe")
            sc = 0
            for hh in range(16):
                bk = 4 + hh // 4
                sc = b.op("vector", lambda e, hh=hh, bk=bk: e.tensor_tensor(
                    abf[:, hh * 128:(hh + 1) * 128], abf[:, hh * 128:(hh + 1) * 128],
                    bank(bk, (hh % 4) * 128, 128), ALU.mult),
                    wait=[("spe", bcm)] if hh == 0 else (), inc="sdv")
            for do in range(4):
                for hh in range(16):
                    b.op("tensor", lambda e, do=do, hh=hh: e.matmul(
                        bank(4 + do), abf[:, hh * 128:(hh + 1) * 128],
                        bwo_sb[:, hh * E + do * 512:hh * E + (do + 1) * 512],
                        start=(hh == 0), stop=(hh == 15)),
                        wait=[("sdv", sc)] if (do == 0 and hh == 0) else (),
                        inc="spe" if (do == 3 and hh == 15) else None)
            wmm = b.now("spe")
            ra = b.op("vector", lambda e, mb=mb: e.tensor_add(
                xint4[mb], xint4[mb], ps[:, 4 * 512:8 * 512]),
                wait=[("spe", wmm), ("qs", xl)], inc="sdv")
            b.dma("sync", d["xres"][(4 + qt) * 128:(5 + qt) * 128, :], xint4[mb],
                  wait=[("sdv", ra)])
        P4_done = b.all_now("sdv", "spe", "qs", "qg", "sac")
        phase_reset()

        if stop_after == "P4":
            b.op("sync", lambda e: e.wait_ge(b.sems["qs"], b.now("qs")))
            b.op("sync", lambda e: e.wait_ge(b.sems["qg"], b.now("qg")))
            b.emit()
            return nc, d

        emit_ffn(PADQ, OWN, False, P4_done)

        b.op("sync", lambda e: e.wait_ge(b.sems["qs"], b.now("qs")))
        b.op("sync", lambda e: e.wait_ge(b.sems["qg"], b.now("qg")))
        b.emit()
    return nc, d


def kernel(**inputs):
    maps = stage_inputs(inputs)
    nc, _ = build_program()
    res = run_bass_kernel_spmd(nc, maps, core_ids=list(range(NCORE)))
    out = np.empty((B, T, E), dtype=np.float32)
    for c in range(NCORE):
        bb, h = c // 2, c % 2
        out[bb, h * 2048:(h + 1) * 2048, :] = res.results[c]["out"]
    return out


# version 19
# speedup vs baseline: 1.3026x; 1.1493x over previous
import sys
import numpy as np

sys.path.insert(0, "/opt/trn_rl_repo")

import ml_dtypes
import concourse.bass as bass
import concourse.mybir as mybir
from concourse.bass_utils import run_bass_kernel_spmd
from contextlib import ExitStack

F32 = mybir.dt.float32
BF16 = mybir.dt.bfloat16
U8 = mybir.dt.uint8
AF = mybir.ActivationFunctionType
ALU = mybir.AluOpType
AX = mybir.AxisListType.X
bfdt = ml_dtypes.bfloat16

E = 2048
T = 4096
B = 4
NCORE = 8
RE = 2560
OWN = 2048
PADQ = 512
KEYS = 4096
LNEPS = 1e-5
HD = 64
NDH = 16
VD = 128
SCALE_A = HD ** -0.5
LAMBDA_INIT = 0.8 - 0.6 * float(np.exp(-0.3 * 12))
RMSEPS = 1e-6
HB = 16
HDB = 128
WIN = 512
SCALE_B = HDB ** -0.5
FI = 8192
NIC = 16
QB_N = 5
KMAX = [min(KEYS, qb * 512 + 2048) for qb in range(QB_N)]

S_E = [4 * m + s for m in range(8) for s in (0, 1)]
S_O = [4 * m + 2 + s for m in range(8) for s in (0, 1)]


def _qperm():
    cols = []
    for c in range(16):
        for s in (S_E[c], S_O[c]):
            cols.extend(range(s * HD, (s + 1) * HD))
    return np.array(cols)


def _npad_host(h):
    q = np.arange(RE)
    if h == 1:
        v = np.zeros(RE)
    else:
        sched = q + 1537.0
        v = np.where(q < PADQ, sched - 1.0, np.minimum(2048.0, sched))
    return np.ascontiguousarray(v.reshape(20, 128).T.astype(np.float32))


def _dmask_host():
    m = np.zeros((4, 128, 1024), dtype=bfdt)
    i = np.arange(128)[:, None]
    j = np.arange(512)[None, :]
    for o in range(4):
        mm = (i <= j - 128 * o).astype(bfdt)
        m[o, :, 0:512] = mm
        m[o, :, 512:1024] = mm
    return m


def _bmask_host(h):
    m = np.zeros((16, 5, 128, 128), dtype=bfdt)
    i = np.arange(128)[:, None]
    j = np.arange(128)[None, :]
    for qt in range(16):
        for et in range(5):
            dd = et * 128 + i - j
            v = (dd >= 0) & (dd <= WIN)
            if h == 0:
                e_abs = (qt + et) * 128 + i
                v = v & np.broadcast_to(e_abs >= PADQ, v.shape)
            m[qt, et] = v.astype(bfdt)
    return m


def _rope_host(h):
    pos = np.maximum(np.arange(KEYS) + (h - 1) * 2048, 0).astype(np.float64)
    inv = 1.0 / 10000.0 ** (np.arange(0, HD, 2) / HD)
    fr = pos[:, None] * inv[None, :]
    fr = fr.reshape(32, 128, 32).transpose(1, 0, 2)
    return (np.ascontiguousarray(np.cos(fr).astype(bfdt)),
            np.ascontiguousarray(np.sin(fr).astype(bfdt)))


def stage_inputs(inputs):
    x = np.ascontiguousarray(inputs["L0_attn_a_x"], dtype=np.float32)
    shared = {
        "wq": np.ascontiguousarray(inputs["L0_attn_a_Wq"][:, _qperm()], dtype=np.float32),
        "wk": np.ascontiguousarray(inputs["L0_attn_a_Wk"], dtype=np.float32),
        "wv": np.ascontiguousarray(inputs["L0_attn_a_Wv"], dtype=np.float32),
        "wo": np.ascontiguousarray(inputs["L0_attn_a_Wo"], dtype=np.float32),
        "bwq": np.ascontiguousarray(inputs["L0_attn_b_wq"], dtype=np.float32),
        "bwk": np.ascontiguousarray(inputs["L0_attn_b_wk"], dtype=np.float32),
        "bwv": np.ascontiguousarray(inputs["L0_attn_b_wv"], dtype=np.float32),
        "bwo": np.ascontiguousarray(inputs["L0_attn_b_wo"], dtype=np.float32),
        "wg": np.ascontiguousarray(inputs["L0_ffn_w_gate"], dtype=np.float32),
        "wu": np.ascontiguousarray(inputs["L0_ffn_w_up"], dtype=np.float32),
        "wd": np.ascontiguousarray(inputs["L0_ffn_w_down"], dtype=np.float32),
        "lq1": np.ascontiguousarray(inputs["L0_attn_a_lam_q1"], dtype=np.float32).reshape(64, 1),
        "lk1": np.ascontiguousarray(inputs["L0_attn_a_lam_k1"], dtype=np.float32).reshape(64, 1),
        "lq2": np.ascontiguousarray(inputs["L0_attn_a_lam_q2"], dtype=np.float32).reshape(64, 1),
        "lk2": np.ascontiguousarray(inputs["L0_attn_a_lam_k2"], dtype=np.float32).reshape(64, 1),
        "g": np.ascontiguousarray(inputs["L0_attn_a_g"], dtype=np.float32).reshape(1, 128),
        "identb": np.eye(128, dtype=bfdt),
        "identf": np.eye(128, dtype=np.float32),
        "dmask": _dmask_host(),
    }
    per_h = {}
    for h in (0, 1):
        rc, rs = _rope_host(h)
        per_h[h] = {"rcos": rc, "rsin": rs, "npad": _npad_host(h),
                    "bmask": _bmask_host(h)}
    maps = []
    for c in range(NCORE):
        bb, h = c // 2, c % 2
        if h == 0:
            xin = np.concatenate([np.zeros((2048, E), np.float32), x[bb, 0:2048]], axis=0)
        else:
            xin = np.ascontiguousarray(x[bb])
        m = {"xin": xin}
        m.update(shared)
        m.update(per_h[h])
        maps.append(m)
    return maps


ENGS = ("sync", "scalar", "vector", "tensor", "gpsimd")


class Bld:
    def __init__(self, nc, ctx):
        self.nc = nc
        self.ops = {e: [] for e in ENGS}
        self.sems = {}
        self.cnt = {}
        self.owner = {}
        self.waited = {e: {} for e in ENGS}
        for name, owner in (("qs", None), ("qg", None), ("spe", "tensor"),
                            ("sdv", "vector"), ("sac", "scalar"), ("spl", "gpsimd")):
            self.sems[name] = ctx.enter_context(nc.semaphore(name))
            self.cnt[name] = 0
            self.owner[name] = owner

    def op(self, eng, fn, wait=(), inc=None, n=1, swait=()):
        wl = []
        for s, v in wait:
            if v <= 0:
                continue
            assert v <= self.cnt[s], f"wait {s}>={v} but only {self.cnt[s]} emitted"
            if self.owner.get(s) == eng:
                continue
            if self.waited[eng].get(s, 0) >= v:
                continue
            self.waited[eng][s] = v
            wl.append((self.sems[s], v))
        for s, v in swait:
            if v <= 0:
                continue
            assert v <= self.cnt[s], f"swait {s}>={v} but only {self.cnt[s]} emitted"
            if self.waited[eng].get(s, 0) >= v:
                continue
            self.waited[eng][s] = v
            wl.append((self.sems[s], v))
        sem = None
        if inc is not None:
            self.cnt[inc] += n
            sem = self.sems[inc]
        self.ops[eng].append((wl, fn, sem, n))
        return self.cnt[inc] if inc else 0

    def dma(self, eng, dst, src, wait=(), **kw):
        q = "qg" if eng == "gpsimd" else "qs"
        return self.op(eng, lambda e: e.dma_start(dst, src, **kw), wait, q, 16)

    def now(self, s):
        return self.cnt[s]

    def all_now(self, *names):
        return [(s, self.cnt[s]) for s in names]

    def emit(self):
        blk_ctx = self.nc.Block()
        blk = blk_ctx.__enter__()
        try:
            for eng, meth in (("gpsimd", blk.gpsimd), ("tensor", blk.tensor),
                              ("vector", blk.vector), ("scalar", blk.scalar),
                              ("sync", blk.sync)):
                ops = self.ops[eng]

                def body(e, ops=ops):
                    for wl, fn, sem, n in ops:
                        for s, v in wl:
                            e.wait_ge(s, v)
                        ins = fn(e)
                        if sem is not None:
                            ins.then_inc(sem, n)
                meth(body)
        finally:
            blk_ctx.__exit__(None, None, None)


def _wsplit(b, eng, dst3, src3, groups, wait=()):
    G = src3.shape[0]
    step = (G + groups - 1) // groups
    r = 0
    for g0 in range(0, G, step):
        g1 = min(G, g0 + step)
        r = b.dma(eng, dst3[:, g0:g1, :], src3[g0:g1].rearrange("g p n -> p g n"),
                  wait=wait if g0 == 0 else ())
    return r


def build_program(stop_after=None, dbg=False):
    nc = bass.Bass("TRN2", target_bir_lowering=False)
    d = {}
    d["xin"] = nc.dram_tensor("xin", [KEYS, E], F32, kind="ExternalInput").ap()
    for n, sh in (("wq", [E, E]), ("wk", [E, 1024]), ("wv", [E, 1024]), ("wo", [E, E]),
                  ("bwq", [E, E]), ("bwk", [E, 512]), ("bwv", [E, 512]), ("bwo", [E, E]),
                  ("wg", [E, FI]), ("wu", [E, FI]), ("wd", [FI, E])):
        d[n] = nc.dram_tensor(n, sh, F32, kind="ExternalInput").ap()
    for n in ("lq1", "lk1", "lq2", "lk2"):
        d[n] = nc.dram_tensor(n, [64, 1], F32, kind="ExternalInput").ap()
    d["g"] = nc.dram_tensor("g", [1, 128], F32, kind="ExternalInput").ap()
    d["identb"] = nc.dram_tensor("identb", [128, 128], BF16, kind="ExternalInput").ap()
    d["identf"] = nc.dram_tensor("identf", [128, 128], F32, kind="ExternalInput").ap()
    d["dmask"] = nc.dram_tensor("dmask", [4, 128, 1024], BF16, kind="ExternalInput").ap()
    d["rcos"] = nc.dram_tensor("rcos", [128, 32, 32], BF16, kind="ExternalInput").ap()
    d["rsin"] = nc.dram_tensor("rsin", [128, 32, 32], BF16, kind="ExternalInput").ap()
    d["npad"] = nc.dram_tensor("npad", [128, 20], F32, kind="ExternalInput").ap()
    d["bmask"] = nc.dram_tensor("bmask", [16, 5, 128, 128], BF16, kind="ExternalInput").ap()
    d["out"] = nc.dram_tensor("out", [OWN, E], F32, kind="ExternalOutput").ap()
    KI = "ExternalOutput" if dbg else "Internal"
    d["xres"] = nc.dram_tensor("xres", [RE, E], F32, kind=KI).ap()
    d["xnt"] = nc.dram_tensor("xnt", [32, 128, E], BF16, kind=KI).ap()
    d["qa"] = nc.dram_tensor("qa", [20, 128, E], BF16, kind=KI).ap()
    d["ka"] = nc.dram_tensor("ka", [32, 128, 1024], BF16, kind=KI).ap()
    d["va"] = nc.dram_tensor("va", [32, 128, 1024], BF16, kind=KI).ap()
    d["anorm"] = nc.dram_tensor("anorm", [16, 128, RE], BF16, kind=KI).ap()
    d["xntb"] = nc.dram_tensor("xntb", [20, 128, E], BF16, kind=KI).ap()
    d["wg16"] = nc.dram_tensor("wg16", [E, FI], BF16, kind="Internal").ap()
    d["wu16"] = nc.dram_tensor("wu16", [E, FI], BF16, kind="Internal").ap()
    d["wd16"] = nc.dram_tensor("wd16", [FI, E], BF16, kind="Internal").ap()
    if dbg:
        d["dbg1"] = nc.dram_tensor("dbg1", [4, 128, E], BF16, kind="ExternalOutput").ap()
        d["dbg2"] = nc.dram_tensor("dbg2", [4, 128, 16], F32, kind="ExternalOutput").ap()

    with ExitStack() as ctx:
        arena = ctx.enter_context(nc.sbuf_tensor("arena", [128, 212832], U8)).ap()
        ps = ctx.enter_context(nc.psum_tensor("ps", [128, 4096], F32)).ap()

        def bank(i, j0=0, n=512):
            return ps[:, i * 512 + j0:i * 512 + j0 + n]

        b = Bld(nc, ctx)

        persist = [212832]

        def palloc(cols, dt, rows=128):
            nb = 4 if dt == F32 else (2 if dt == BF16 else 1)
            persist[0] -= cols * nb
            persist[0] -= persist[0] % 32
            return arena[0:rows, persist[0]:persist[0] + cols * nb].bitcast(dt)

        identb = palloc(128, BF16)
        identf = palloc(128, F32)
        geff = palloc(128, F32)
        nlam = palloc(1, F32)
        npadc = palloc(20, F32)
        onesb = palloc(1, BF16)
        onesf = palloc(1, F32)
        onesrow = palloc(128, F32, rows=1)
        epsb = palloc(1, F32)
        smallf = palloc(16, F32)
        lnst = palloc(16, F32)
        PBASE = persist[0]

        cur = [0]

        def alloc(cols, dt, rows=128):
            nb = 4 if dt == F32 else (2 if dt == BF16 else 1)
            cur[0] = (cur[0] + 31) // 32 * 32
            r = arena[0:rows, cur[0]:cur[0] + cols * nb].bitcast(dt)
            cur[0] += cols * nb
            return r

        def phase_reset(floor=0):
            assert cur[0] <= PBASE, f"arena overflow: {cur[0]} > {PBASE}"
            cur[0] = floor

        b.dma("gpsimd", identb, d["identb"])
        b.dma("gpsimd", identf, d["identf"])
        b.dma("gpsimd", npadc, d["npad"])
        lamv = [alloc(1, F32, rows=64) for _ in range(4)]
        for i, n in enumerate(("lq1", "lk1", "lq2", "lk2")):
            b.dma("gpsimd", lamv[i], d[n])
        grow = alloc(128, F32, rows=1)
        b.dma("gpsimd", grow, d["g"])
        b.op("gpsimd", lambda e: e.memset(onesb, 1.0), inc="spl")
        b.op("gpsimd", lambda e: e.memset(onesf, 1.0), inc="spl")
        b.op("gpsimd", lambda e: e.memset(onesrow, 1.0), inc="spl")
        b.op("gpsimd", lambda e: e.memset(epsb, LNEPS), inc="spl")
        b.op("tensor", lambda e: e.matmul(bank(0, 0, 1)[0:1, :], lamv[0], lamv[1],
                                          start=True, stop=True), wait=b.all_now("qg", "spl"))
        b.op("tensor", lambda e: e.matmul(bank(0, 1, 1)[0:1, :], lamv[2], lamv[3],
                                          start=True, stop=True), inc="spe")
        b.op("scalar", lambda e: e.activation(smallf[0:1, 0:2], bank(0, 0, 2)[0:1, :], AF.Exp),
             wait=b.all_now("spe"), inc="sac")
        h0 = b.op("vector", lambda e: e.tensor_sub(smallf[0:1, 2:3], smallf[0:1, 0:1], smallf[0:1, 1:2]),
                  wait=b.all_now("sac"), inc="sdv")
        b.op("vector", lambda e: e.tensor_scalar(smallf[0:1, 3:4], smallf[0:1, 2:3],
                                                 LAMBDA_INIT, -1.0, ALU.add, ALU.mult),
             swait=[("sdv", h0)], inc="sdv")
        b.op("tensor", lambda e: e.matmul(bank(1, 0, 1), onesrow, smallf[0:1, 3:4],
                                          start=True, stop=True), wait=b.all_now("sdv"))
        b.op("tensor", lambda e: e.matmul(bank(2, 0, 128), onesrow, grow,
                                          start=True, stop=True), inc="spe")
        b.op("vector", lambda e: e.tensor_copy(nlam, bank(1, 0, 1)), wait=b.all_now("spe"))
        b.op("vector", lambda e: e.tensor_scalar_mul(geff, bank(2, 0, 128), 1.0 - LAMBDA_INIT),
             inc="sdv")
        P0_done = b.all_now("sdv", "spl", "qg")

        def emit_ln(src, sqdump, xn_dst, wait_ld=(), war_apply=()):
            b.op("vector", lambda e: e.reduce_sum(lnst[:, 0:1], src, AX), wait=wait_ld)
            f0 = b.op("vector", lambda e: e.scalar_tensor_tensor(
                sqdump, src, 1.0, src, ALU.mult, ALU.mult, accum_out=lnst[:, 1:2]),
                wait=war_apply, inc="sdv")
            b.op("vector", lambda e: e.tensor_scalar_mul(lnst[:, 2:3], lnst[:, 0:1], 1.0 / E),
                 swait=[("sdv", f0)])
            f1 = b.op("vector", lambda e: e.tensor_scalar_mul(lnst[:, 3:4], lnst[:, 1:2], 1.0 / E),
                      inc="sdv")
            f2 = b.op("vector", lambda e: e.tensor_mul(lnst[:, 4:5], lnst[:, 2:3], lnst[:, 2:3]),
                      swait=[("sdv", f1)], inc="sdv")
            b.op("vector", lambda e: e.tensor_sub(lnst[:, 5:6], lnst[:, 3:4], lnst[:, 4:5]),
                 swait=[("sdv", f2)], inc="sdv")
            b.op("scalar", lambda e: e.activation(lnst[:, 6:7], lnst[:, 5:6], AF.Sqrt,
                                                  bias=epsb), wait=b.all_now("sdv"), inc="sac")
            r1 = b.op("vector", lambda e: e.reciprocal(lnst[:, 7:8], lnst[:, 6:7]),
                      wait=b.all_now("sac"), inc="sdv")
            r2 = b.op("vector", lambda e: e.tensor_scalar(
                lnst[:, 8:9], lnst[:, 2:3], lnst[:, 7:8], -1.0, ALU.mult, ALU.mult),
                swait=[("sdv", r1)], inc="sdv")
            return b.op("vector", lambda e: e.tensor_scalar(
                xn_dst, src, lnst[:, 7:8], lnst[:, 8:9], ALU.mult, ALU.add),
                swait=[("sdv", r2)], inc="sdv")

        def emit_rope_tm(buf, nh, cosap, sinap, tmp):
            v3 = buf.rearrange("p (c n) -> p c n", c=nh)
            x1 = v3[:, :, 0:32]
            x2 = v3[:, :, 32:64]
            cb = cosap.unsqueeze(1).broadcast_to([128, nh, 32])
            sb = sinap.unsqueeze(1).broadcast_to([128, nh, 32])
            tv = [tmp[:, i * nh * 32:(i + 1) * nh * 32].rearrange(
                "p (c n) -> p c n", c=nh) for i in range(3)]
            b.op("vector", lambda e: e.tensor_tensor(tv[0], x1, cb, ALU.mult))
            b.op("vector", lambda e: e.tensor_tensor(tv[1], x1, sb, ALU.mult))
            f0 = b.op("vector", lambda e: e.tensor_tensor(tv[2], x2, sb, ALU.mult),
                      inc="sdv")
            b.op("vector", lambda e: e.tensor_add(x1, tv[0], tv[2]), swait=[("sdv", f0)])
            f1 = b.op("vector", lambda e: e.tensor_tensor(tv[0], x2, cb, ALU.mult),
                      inc="sdv")
            return b.op("vector", lambda e: e.tensor_sub(x2, tv[0], tv[1]),
                        swait=[("sdv", f1)], inc="sdv")

        wk_sb = alloc(16 * 1024, BF16)
        wv_sb = alloc(16 * 1024, BF16)
        rct = alloc(32 * 32, BF16)
        rst = alloc(32 * 32, BF16)
        xbf = [alloc(E, BF16) for _ in range(2)]
        xn = [alloc(E, BF16) for _ in range(2)]
        xnt_sb = [alloc(16 * 128, BF16) for _ in range(2)]
        dbgsnap = alloc(64, F32)
        dbgx = alloc(4 * E, BF16)
        cvst = [alloc(16 * 512, BF16) for _ in range(2)]
        ktm = alloc(1024, BF16)
        ktf = alloc(1024, BF16)
        rtmp = alloc(3 * 1024, BF16)
        vst = alloc(1024, BF16)

        _wsplit(b, "gpsimd", wk_sb.rearrange("p (g n) -> p g n", g=16),
                d["wk"].rearrange("(g p) n -> g p n", p=128), 4, wait=P0_done)
        _wsplit(b, "gpsimd", wv_sb.rearrange("p (g n) -> p g n", g=16),
                d["wv"].rearrange("(g p) n -> g p n", p=128), 4)
        b.dma("gpsimd", rct, d["rcos"].rearrange("p a b -> p (a b)"))
        wkv_ready = b.dma("gpsimd", rst, d["rsin"].rearrange("p a b -> p (a b)"))
        rc3 = rct.rearrange("p (a b) -> p a b", a=32)
        rs3 = rst.rearrange("p (a b) -> p a b", a=32)

        CONV = []
        for nm in ("wg", "wu"):
            for c in range(16):
                CONV.append((nm, nm + "16", c))
        for c in range(16):
            CONV.append(("wd", "wd16", c))
        conv_i = [0]
        conv_in_mark = {}
        CONV_DONE = [0]

        def conv_src(nm, c):
            if nm in ("wd", "wd16"):
                return d[nm][c * 512:(c + 1) * 512, :].rearrange("(g p) n -> g p n", p=128)
            return d[nm][:, c * 512:(c + 1) * 512].rearrange("(g p) n -> g p n", p=128)

        def emit_conv_pair():
            i = conv_i[0]
            if i >= len(CONV) + 1:
                return
            if i >= 1:
                nm, dn, c = CONV[i - 1]
                dst3 = conv_src(dn, c)
                g = dst3.shape[0]
                CONV_DONE[0] = b.dma(
                    "gpsimd", dst3.rearrange("g p n -> p g n"),
                    cvst[(i - 1) % 2].rearrange("p (g n) -> p g n", g=g),
                    wait=[("qg", conv_in_mark[i - 1])])
            if i < len(CONV):
                nm, dn, c = CONV[i]
                src3 = conv_src(nm, c)
                g = src3.shape[0]
                w = [("qg", CONV_DONE[0])] if i >= 2 else ()
                conv_in_mark[i] = b.dma(
                    "gpsimd", cvst[i % 2].rearrange("p (g n) -> p g n", g=g),
                    src3.rearrange("g p n -> p g n"), wait=w)
            conv_i[0] = i + 1

        for kt in range(32):
            tb = kt % 2
            xt = xnt_sb[tb]
            war = b.all_now("spe") if kt >= 2 else list(P0_done)
            ld = b.dma("gpsimd", xbf[tb], d["xin"][kt * 128:(kt + 1) * 128, :],
                       wait=war)
            emit_conv_pair()
            emit_ln(xbf[tb], xn[1 - tb], xn[tb], wait_ld=[("qg", ld)],
                    war_apply=b.all_now("spe"))
            xnw = b.now("sdv")
            for c in range(16):
                b.op("tensor", lambda e, c=c, tb=tb: e.matmul(
                    bank(c // 4, (c % 4) * 128, 128), xn[tb][:, c * 128:(c + 1) * 128],
                    identb, start=True, stop=True),
                    wait=([("sdv", xnw), ("qg", wkv_ready)]) if c == 0 else (),
                    inc="spe" if c == 15 else None)
            ev = b.op("vector", lambda e, xt=xt: e.tensor_copy(xt, ps[:, 0:2048]),
                      wait=b.all_now("spe") + [("qs", b.now("qs"))], inc="sdv")
            b.dma("sync", d["xnt"][kt], xt, wait=[("sdv", ev)])
            if dbg and kt < 4:
                b.op("vector", lambda e, kt=kt: e.tensor_copy(dbgsnap[:, kt * 16:(kt + 1) * 16], lnst))
                snp = b.op("vector", lambda e, kt=kt, tb=tb: e.tensor_copy(
                    dbgx[:, kt * E:(kt + 1) * E], xn[tb]), inc="sdv")
                b.dma("sync", d["dbg1"][kt], dbgx[:, kt * E:(kt + 1) * E], wait=[("sdv", snp)])
                b.dma("sync", d["dbg2"][kt], dbgsnap[:, kt * 16:(kt + 1) * 16], wait=[("sdv", snp)])
            for half in range(2):
                for ic in range(16):
                    b.op("tensor", lambda e, half=half, ic=ic, xt=xt: e.matmul(
                        bank(4 + half), xt[:, ic * 128:(ic + 1) * 128],
                        wk_sb[:, ic * 1024 + half * 512:ic * 1024 + (half + 1) * 512],
                        start=(ic == 0), stop=(ic == 15)),
                        wait=[("sdv", ev)] if (half == 0 and ic == 0) else (),
                        inc="spe" if (half == 1 and ic == 15) else None)
            b.op("vector", lambda e: e.tensor_copy(ktm, ps[:, 4 * 512:6 * 512]),
                 wait=b.all_now("spe"), inc="sdv")
            emit_rope_tm(ktm, 16, rc3[:, kt, :], rs3[:, kt, :], rtmp)
            krp = b.now("sdv")
            for c in range(8):
                b.op("tensor", lambda e, c=c: e.matmul(
                    bank(6 + c // 4, (c % 4) * 128, 128), ktm[:, c * 128:(c + 1) * 128],
                    identb, start=True, stop=True),
                    wait=[("sdv", krp)] if c == 0 else (),
                    inc="spe" if c == 7 else None)
            kev = b.op("vector", lambda e: e.tensor_copy(ktf, ps[:, 6 * 512:8 * 512]),
                       wait=b.all_now("spe") + [("qs", b.now("qs"))], inc="sdv")
            b.dma("sync", d["ka"][kt], ktf, wait=[("sdv", kev)])
            for half in range(2):
                for ic in range(16):
                    b.op("tensor", lambda e, half=half, ic=ic, xt=xt: e.matmul(
                        bank(4 + half), xt[:, ic * 128:(ic + 1) * 128],
                        wv_sb[:, ic * 1024 + half * 512:ic * 1024 + (half + 1) * 512],
                        start=(ic == 0), stop=(ic == 15)),
                        wait=[("sdv", b.now("sdv"))] if (half == 0 and ic == 0) else (),
                        inc="spe" if (half == 1 and ic == 15) else None)
            b.op("vector", lambda e: e.tensor_copy(vst, ps[:, 4 * 512:6 * 512]),
                 wait=b.all_now("spe") + [("qs", b.now("qs"))], inc="sdv")
            b.dma("sync", d["va"][kt], vst, wait=b.all_now("sdv"))
        P1a_done = b.all_now("sdv", "spe", "qs", "qg", "sac")
        phase_reset()

        if stop_after == "P1a":
            b.op("sync", lambda e: e.wait_ge(b.sems["qs"], b.now("qs")))
            b.op("sync", lambda e: e.wait_ge(b.sems["qg"], b.now("qg")))
            b.emit()
            return nc, d

        wq_sb = alloc(16 * E, BF16)
        rctq = alloc(20 * 32, BF16)
        rstq = alloc(20 * 32, BF16)
        xntq = [alloc(E, BF16) for _ in range(2)]
        qtm = alloc(E, BF16)
        qtf = alloc(E, BF16)
        rtmpQ = alloc(3 * 2048, BF16)
        _wsplit(b, "gpsimd", wq_sb.rearrange("p (g n) -> p g n", g=16),
                d["wq"].rearrange("(g p) n -> g p n", p=128), 4, wait=P1a_done)
        b.dma("gpsimd", rctq, d["rcos"][:, 12:32, :].rearrange("p a b -> p (a b)"))
        wq_ready = b.dma("gpsimd", rstq, d["rsin"][:, 12:32, :].rearrange("p a b -> p (a b)"))
        rcq3 = rctq.rearrange("p (a b) -> p a b", a=20)
        rsq3 = rstq.rearrange("p (a b) -> p a b", a=20)

        for qt in range(20):
            emit_conv_pair()
            tb = qt % 2
            ld = b.dma("sync", xntq[tb], d["xnt"][12 + qt],
                       wait=(list(P1a_done) + [("sdv", b.now("sdv"))]) if qt < 2
                       else [("spe", b.now("spe"))])
            for oc4 in range(4):
                for ic in range(16):
                    b.op("tensor", lambda e, oc4=oc4, ic=ic, tb=tb: e.matmul(
                        bank(oc4), xntq[tb][:, ic * 128:(ic + 1) * 128],
                        wq_sb[:, ic * E + oc4 * 512:ic * E + (oc4 + 1) * 512],
                        start=(ic == 0), stop=(ic == 15)),
                        wait=[("qs", ld), ("qg", wq_ready), ("sdv", b.now("sdv"))]
                        if (oc4 == 0 and ic == 0) else (),
                        inc="spe" if (oc4 == 3 and ic == 15) else None)
            b.op("vector", lambda e: e.tensor_copy(qtm, ps[:, 0:2048]),
                 wait=b.all_now("spe"), inc="sdv")
            emit_rope_tm(qtm, 32, rcq3[:, qt, :], rsq3[:, qt, :], rtmpQ)
            qrp = b.now("sdv")
            for c in range(16):
                b.op("tensor", lambda e, c=c: e.matmul(
                    bank(4 + c // 4, (c % 4) * 128, 128), qtm[:, c * 128:(c + 1) * 128],
                    identb, start=True, stop=True),
                    wait=[("sdv", qrp)] if c == 0 else (),
                    inc="spe" if c == 15 else None)
            qev = b.op("vector", lambda e: e.tensor_copy(qtf, ps[:, 4 * 512:8 * 512]),
                       wait=b.all_now("spe") + [("qs", b.now("qs"))], inc="sdv")
            b.dma("sync", d["qa"][qt], qtf, wait=[("sdv", qev)])
        P1b_done = b.all_now("sdv", "spe", "qs", "qg")
        phase_reset()

        if stop_after == "P1b":
            b.op("sync", lambda e: e.wait_ge(b.sems["qs"], b.now("qs")))
            b.op("sync", lambda e: e.wait_ge(b.sems["qg"], b.now("qg")))
            b.emit()
            return nc, d

        ka_sb = alloc(32 * 1024, BF16)
        va_sb = alloc(32 * 1024, BF16)
        q_sb = [alloc(4 * E, BF16) for _ in range(2)]
        p_sb = [alloc(1024, BF16) for _ in range(2)]
        dmaskb = alloc(4 * 1024, BF16)
        l_sb = alloc(1024, F32, rows=1)
        lcol = alloc(16, F32)
        a0f = alloc(512, F32)
        a1f = alloc(512, F32)
        attf = alloc(128, F32)
        sqd = alloc(128, F32)
        anq = alloc(128, BF16)
        an_sb = alloc(512, BF16)
        rmss = alloc(8, F32)

        _wsplit(b, "gpsimd", ka_sb.rearrange("p (g n) -> p g n", g=32), d["ka"], 8,
                wait=P1b_done)
        _wsplit(b, "gpsimd", va_sb.rearrange("p (g n) -> p g n", g=32), d["va"], 8)
        kav_ready = b.dma("gpsimd", dmaskb, d["dmask"].rearrange("g p n -> p g n"))

        pe_qb_mark = {}
        for qb in range(QB_N):
            qsb = q_sb[qb % 2]
            qw = [("qg", kav_ready)]
            if qb >= 2:
                qw.append(("spe", pe_qb_mark[qb - 2]))
            qld = _wsplit(b, "sync", qsb.rearrange("p (g n) -> p g n", g=4),
                          d["qa"][qb * 4:(qb + 1) * 4], 2, wait=qw)
            q3 = qsb.rearrange("p (t c n) -> p t c n", t=4, c=16)
            nkt = KMAX[qb] // 128
            for dh in range(NDH):
                kc, base = dh // 2, 64 * (dh % 2)
                qc = 2 * (dh // 2)
                vcol = (dh // 2) * 128
                exp_mark = {}
                pe_pmark = {}
                mask_dv = {}

                def emit_lav(kt):
                    P = kt % 2
                    pb = p_sb[P]
                    wav = ([("sdv", mask_dv[kt])] if mask_dv.get(kt)
                           else [("sac", exp_mark[kt])])
                    b.op("tensor", lambda e, pb=pb, kt=kt: e.matmul(
                        bank(6, 0, 512)[0:1, :], onesb, pb[:, 0:512],
                        start=(kt == 0), stop=(kt == nkt - 1)), wait=wav)
                    b.op("tensor", lambda e, pb=pb, kt=kt: e.matmul(
                        bank(7, 0, 512)[0:1, :], onesb, pb[:, 512:1024],
                        start=(kt == 0), stop=(kt == nkt - 1)))
                    b.op("tensor", lambda e, pb=pb, kt=kt, vcol=vcol: e.matmul(
                        bank(4), va_sb[:, kt * 1024 + vcol:kt * 1024 + vcol + 128],
                        pb[:, 0:512], start=(kt == 0), stop=(kt == nkt - 1)))
                    pe_pmark[kt] = b.op("tensor", lambda e, pb=pb, kt=kt, vcol=vcol: e.matmul(
                        bank(5), va_sb[:, kt * 1024 + vcol:kt * 1024 + vcol + 128],
                        pb[:, 512:1024], start=(kt == 0), stop=(kt == nkt - 1)), inc="spe")

                for kt in range(nkt):
                    P = kt % 2
                    sb0 = P * 2
                    pb = p_sb[P]
                    w = [("qs", qld)]
                    if kt >= 2:
                        w.append(("sac", exp_mark[kt - 2]))
                    if kt == 0:
                        w.append(("sdv", b.now("sdv")))
                        w.append(("sac", b.now("sac")))
                    b.op("tensor", lambda e, kc=kc, base=base, kt=kt, sb0=sb0, qc=qc, q3=q3: e.matmul(
                        bank(sb0),
                        ka_sb[base:base + 64, kt * 1024 + kc * 128:kt * 1024 + (kc + 1) * 128],
                        q3[base:base + 64, :, qc, :],
                        start=True, stop=True), wait=w)
                    b.op("tensor", lambda e, kc=kc, base=base, kt=kt, sb0=sb0, qc=qc, q3=q3: e.matmul(
                        bank(sb0 + 1),
                        ka_sb[base:base + 64, kt * 1024 + kc * 128:kt * 1024 + (kc + 1) * 128],
                        q3[base:base + 64, :, qc + 1, :],
                        start=True, stop=True), inc="spe")
                    smm = b.now("spe")
                    wexp = [("spe", smm)]
                    if kt >= 2:
                        wexp.append(("spe", pe_pmark[kt - 2]))
                    exp_mark[kt] = b.op("scalar", lambda e, sb0=sb0, pb=pb: e.activation(
                        pb, ps[:, sb0 * 512:(sb0 + 2) * 512], AF.Exp, scale=SCALE_A),
                        wait=wexp, inc="sac")
                    o = kt - (qb * 4 + 12)
                    if 0 <= o <= 3:
                        mask_dv[kt] = b.op("vector", lambda e, pb=pb, o=o: e.tensor_mul(
                            pb, pb, dmaskb[:, o * 1024:(o + 1) * 1024]),
                            wait=[("sac", exp_mark[kt])], inc="sdv")
                    if kt >= 1:
                        emit_lav(kt - 1)
                emit_lav(nkt - 1)
                kdone = b.now("spe")
                b.op("vector", lambda e: e.tensor_copy(l_sb, ps[0:1, 6 * 512:8 * 512]),
                     wait=[("spe", kdone)])
                b.op("vector", lambda e: e.tensor_copy(a0f, bank(4)))
                aev = b.op("vector", lambda e: e.tensor_copy(a1f, bank(5)), inc="sdv")
                for s in range(2):
                    for q4 in range(4):
                        b.op("tensor", lambda e, s=s, q4=q4: e.matmul(
                            bank(2, s * 4 + q4, 1),
                            l_sb[0:1, s * 512 + q4 * 128:s * 512 + (q4 + 1) * 128],
                            onesf[0:1, 0:1], start=True, stop=True),
                            wait=[("sdv", aev)] if (s == 0 and q4 == 0) else (),
                            inc="spe" if (s == 1 and q4 == 3) else None)
                lmm = b.now("spe")
                g0 = b.op("vector", lambda e: e.tensor_copy(lcol[:, 0:8], bank(2, 0, 8)),
                          wait=[("spe", lmm)], inc="sdv")
                b.op("vector", lambda e, qb=qb: e.tensor_sub(
                    lcol[:, 0:4], lcol[:, 0:4], npadc[:, qb * 4:(qb + 1) * 4]),
                    swait=[("sdv", g0)])
                g1 = b.op("vector", lambda e, qb=qb: e.tensor_sub(
                    lcol[:, 4:8], lcol[:, 4:8], npadc[:, qb * 4:(qb + 1) * 4]), inc="sdv")
                g2 = b.op("vector", lambda e: e.reciprocal(lcol[:, 8:16], lcol[:, 0:8]),
                          swait=[("sdv", g1)], inc="sdv")
                b.op("vector", lambda e: e.tensor_tensor(
                    lcol[:, 4:8], lcol[:, 12:16], nlam.broadcast_to([128, 4]), ALU.mult),
                    swait=[("sdv", g2)], inc="sdv")
                linv = b.now("sdv")
                for q4 in range(4):
                    b.op("tensor", lambda e, q4=q4: e.transpose(
                        bank(0, q4 * 128, 128), a0f[:, q4 * 128:(q4 + 1) * 128], identf),
                        wait=[("sdv", aev), ("sac", b.now("sac"))] if q4 == 0 else ())
                    b.op("tensor", lambda e, q4=q4: e.transpose(
                        bank(1, q4 * 128, 128), a1f[:, q4 * 128:(q4 + 1) * 128], identf),
                        inc="spe" if q4 == 3 else None)
                tdone = b.now("spe")
                sqm = 0
                btm = 0
                for q4 in range(4):
                    ww = [("spe", tdone), ("sdv", linv)]
                    if sqm:
                        ww.append(("sac", sqm))
                    b.op("vector", lambda e, q4=q4: e.tensor_scalar(
                        attf, bank(0, q4 * 128, 128), lcol[:, 8 + q4:9 + q4], None, ALU.mult),
                        wait=ww, swait=[("sdv", linv)])
                    b.op("vector", lambda e, q4=q4: e.scalar_tensor_tensor(
                        attf, bank(1, q4 * 128, 128), lcol[:, 4 + q4:5 + q4], attf,
                        ALU.mult, ALU.add), inc="sdv")
                    sqm = b.op("scalar", lambda e: e.activation(sqd, attf, AF.Square,
                                                                accum_out=rmss[:, 0:1]),
                               wait=b.all_now("sdv"), inc="sac")
                    b.op("vector", lambda e: e.tensor_scalar(
                        rmss[:, 1:2], rmss[:, 0:1], 1.0 / VD, RMSEPS, ALU.mult, ALU.add),
                        wait=[("sac", sqm)], inc="sdv")
                    b.op("scalar", lambda e: e.activation(rmss[:, 2:3], rmss[:, 1:2], AF.Sqrt),
                         wait=b.all_now("sdv"), inc="sac")
                    r3 = b.op("vector", lambda e: e.reciprocal(rmss[:, 3:4], rmss[:, 2:3]),
                              wait=b.all_now("sac"), inc="sdv")
                    wanq = [("spe", btm)] if btm else []
                    b.op("vector", lambda e: e.scalar_tensor_tensor(
                        anq, attf, rmss[:, 3:4], geff, ALU.mult, ALU.mult),
                        wait=wanq, swait=[("sdv", r3)], inc="sdv")
                    btm = b.op("tensor", lambda e, q4=q4: e.matmul(
                        bank(3, q4 * 128, 128), anq, identb, start=True, stop=True),
                        wait=b.all_now("sdv"), inc="spe")
                ev2 = b.op("vector", lambda e: e.tensor_copy(an_sb, bank(3)),
                           wait=[("spe", btm), ("qs", b.now("qs"))], inc="sdv")
                b.dma("sync", d["anorm"][dh][:, qb * 512:(qb + 1) * 512], an_sb,
                      wait=[("sdv", ev2)])
            pe_qb_mark[qb] = b.now("spe")
        P2_done = b.all_now("sdv", "spe", "qs", "qg", "sac")
        phase_reset()

        if stop_after == "P2":
            b.op("sync", lambda e: e.wait_ge(b.sems["qs"], b.now("qs")))
            b.op("sync", lambda e: e.wait_ge(b.sems["qg"], b.now("qg")))
            b.emit()
            return nc, d

        wo_sb = alloc(16 * E, BF16)
        an16 = [alloc(16 * 512, BF16) for _ in range(2)]
        xint = [alloc(E, F32) for _ in range(2)]
        xrout = [alloc(E, F32) for _ in range(2)]
        _wsplit(b, "gpsimd", wo_sb.rearrange("p (g n) -> p g n", g=16),
                d["wo"].rearrange("(g p) n -> g p n", p=128), 4, wait=P2_done)
        wo_ready = b.now("qg")
        qb_pe = {}
        for qb in range(QB_N):
            sb = qb % 2
            aw = list(P2_done)
            if qb >= 2:
                aw.append(("spe", qb_pe[qb - 2]))
            anl = _wsplit(b, "sync", an16[sb].rearrange("p (g n) -> p g n", g=16),
                          d["anorm"][:, :, qb * 512:(qb + 1) * 512], 4, wait=aw)
            for q4 in range(4):
                tb = q4 % 2
                qt = qb * 4 + q4
                xl = b.dma("sync", xint[tb], d["xin"][(12 + qt) * 128:(13 + qt) * 128, :],
                           wait=b.all_now("sdv"))
                for do in range(4):
                    for dh in range(16):
                        b.op("tensor", lambda e, do=do, dh=dh, q4=q4, sb=sb: e.matmul(
                            bank(do), an16[sb][:, dh * 512 + q4 * 128:dh * 512 + (q4 + 1) * 128],
                            wo_sb[:, dh * E + do * 512:dh * E + (do + 1) * 512],
                            start=(dh == 0), stop=(dh == 15)),
                            wait=[("qs", anl), ("qg", wo_ready), ("sdv", b.now("sdv"))]
                            if (do == 0 and dh == 0) else (),
                            inc="spe" if (do == 3 and dh == 15) else None)
                mm = b.now("spe")
                ra = b.op("vector", lambda e, tb=tb: e.tensor_add(
                    xrout[tb], xint[tb], ps[:, 0:2048]),
                    wait=[("spe", mm), ("qs", xl)], inc="sdv")
                b.dma("sync", d["xres"][qt * 128:(qt + 1) * 128, :], xrout[tb],
                      wait=[("sdv", ra)])
            qb_pe[qb] = b.now("spe")
        P2b_done = b.all_now("sdv", "spe", "qs", "qg")
        phase_reset()

        if stop_after == "P2b":
            b.op("sync", lambda e: e.wait_ge(b.sems["qs"], b.now("qs")))
            b.op("sync", lambda e: e.wait_ge(b.sems["qg"], b.now("qg")))
            b.emit()
            return nc, d

        def emit_ffn_group(g0, gn, to_xres, wait0):
            nt = gn // 128
            xnt_g = alloc(16 * gn, BF16)
            wgb = [alloc(16 * 512, BF16) for _ in range(2)]
            wub = [alloc(16 * 512, BF16) for _ in range(2)]
            wdb = alloc(4 * 2048, BF16)
            acc = alloc(nt * 2048, F32)
            xint2_t = alloc(E, F32)
            xint2 = [xint2_t, xint2_t]
            xn2 = [alloc(E, BF16) for _ in range(2)]
            sg2 = [alloc(512, BF16) for _ in range(2)]
            hb2 = [alloc(512, BF16) for _ in range(2)]
            hsp2 = [alloc(512, BF16) for _ in range(2)]
            hspT2 = [alloc(512, BF16) for _ in range(2)]
            sp3 = [alloc(512, F32), alloc(512, F32), alloc(384, F32)]

            ln_ev = []
            for t in range(nt):
                tb = t % 2
                war = [("sdv", ln_ev[-1])] if ln_ev else list(wait0)
                ld = b.dma("sync", xint2[tb], d["xres"][g0 + t * 128:g0 + (t + 1) * 128, :],
                           wait=war)
                emit_ln(xint2[tb], xn2[1 - tb], xn2[tb], wait_ld=[("qs", ld)],
                        war_apply=b.all_now("spe"))
                xnw = b.now("sdv")
                b0 = (t % 2) * 4
                for c in range(16):
                    bk = b0 + c // 4
                    b.op("tensor", lambda e, c=c, bk=bk, tb=tb: e.matmul(
                        bank(bk, (c % 4) * 128, 128), xn2[tb][:, c * 128:(c + 1) * 128],
                        identb, start=True, stop=True),
                        wait=([("sdv", xnw)] + ([("sdv", ln_ev[-1])] if ln_ev else []))
                        if c == 0 else (), inc="spe" if c == 15 else None)
                ev = b.op("vector", lambda e, t=t, b0=b0: e.tensor_copy(
                    xnt_g[:, t * 2048:(t + 1) * 2048], ps[:, b0 * 512:(b0 + 4) * 512]),
                    wait=b.all_now("spe"), inc="sdv")
                ln_ev.append(ev)
            ln_done = b.now("sdv")
            xg3 = xnt_g.rearrange("p (t c n) -> p t c n", t=nt, c=16)

            silu_mark = {}
            evT_mark = {}
            acc_mark = {}
            hm_mark = {}
            trm_mark = {}
            dmm_mark = {}
            pe_chunk = {}
            for ic in range(NIC):
                buf = ic % 2
                wwait = [("sdv", ln_done)]
                if ic >= 2:
                    wwait.append(("spe", pe_chunk[ic - 2]))
                wwait.append(("qg", CONV_DONE[0]))
                _wsplit(b, "gpsimd", wgb[buf].rearrange("p (g n) -> p g n", g=16),
                        d["wg16"][:, ic * 512:(ic + 1) * 512].rearrange("(g p) n -> g p n", p=128),
                        2, wait=wwait)
                wgu = _wsplit(b, "gpsimd", wub[buf].rearrange("p (g n) -> p g n", g=16),
                        d["wu16"][:, ic * 512:(ic + 1) * 512].rearrange("(g p) n -> g p n", p=128), 2)
                wdv3 = wdb.rearrange("p (g n) -> p g n", g=4)
                wdw = [("spe", pe_chunk[ic - 1])] if ic >= 1 else ()
                wld0 = _wsplit(b, "gpsimd", wdv3[:, :, 0:1024],
                               d["wd16"][ic * 512:(ic + 1) * 512, 0:1024].rearrange("(g p) n -> g p n", p=128),
                               1, wait=wdw)
                wld = _wsplit(b, "gpsimd", wdv3[:, :, 1024:2048],
                              d["wd16"][ic * 512:(ic + 1) * 512, 1024:2048].rearrange("(g p) n -> g p n", p=128), 1)
                def emit_front(t):
                    tb = t % 2
                    pA = (t % 2) * 2
                    key = (ic, t)
                    pkey = (ic, t - 2) if t >= 2 else ((ic - 1, nt - 2 + t) if ic >= 1 else None)
                    w = [("qg", wgu)]
                    if pkey is not None and pkey in evT_mark:
                        w.append(("sac", evT_mark[pkey]))
                        w.append(("sdv", hm_mark[pkey]))
                    for rc in range(16):
                        b.op("tensor", lambda e, rc=rc, t=t, pA=pA, buf=buf: e.matmul(
                            bank(pA), xg3[:, t, rc, :], wgb[buf][:, rc * 512:(rc + 1) * 512],
                            start=(rc == 0), stop=(rc == 15)), wait=w if rc == 0 else ())
                        b.op("tensor", lambda e, rc=rc, t=t, pA=pA, buf=buf: e.matmul(
                            bank(pA + 1), xg3[:, t, rc, :], wub[buf][:, rc * 512:(rc + 1) * 512],
                            start=(rc == 0), stop=(rc == 15)),
                            inc="spe" if rc == 15 else None)
                    gmm = b.now("spe")
                    wsilu = [("spe", gmm)]
                    if pkey is not None and pkey in hm_mark:
                        wsilu.append(("sdv", hm_mark[pkey]))
                    silu_mark[key] = b.op("scalar", lambda e, tb=tb, pA=pA: e.activation(
                        sg2[tb], bank(pA), AF.Silu), wait=wsilu, inc="sac")
                    wh = [("sac", silu_mark[key])]
                    fh = b.op("vector", lambda e, tb=tb, pA=pA: e.tensor_tensor(
                        hb2[tb], sg2[tb], bank(pA + 1), ALU.mult),
                        wait=wh, inc="sdv")
                    h4 = hb2[tb].rearrange("p (n f) -> p n f", f=4)
                    hs4 = hsp2[tb].rearrange("p (n f) -> p n f", f=4)
                    a = lambda i: sp3[0][:, i * 128:(i + 1) * 128]
                    mm_ = lambda i: sp3[1][:, i * 128:(i + 1) * 128]
                    z = lambda i: sp3[2][:, i * 128:(i + 1) * 128]
                    wp = []
                    if pkey is not None and pkey in trm_mark:
                        wp.append(("spe", trm_mark[pkey]))
                    for i in range(4):
                        b.op("vector", lambda e, i=i, h4=h4, a=a: e.tensor_mul(
                            a(i), h4[:, :, i], h4[:, :, i]),
                            wait=wp if i == 0 else (), swait=[("sdv", fh)] if i == 0 else (),
                            inc="sdv" if i == 3 else None)
                    fa = b.now("sdv")
                    b.op("vector", lambda e, a=a, m=mm_: e.tensor_tensor(m(0), a(0), a(1), ALU.min),
                         swait=[("sdv", fa)])
                    b.op("vector", lambda e, a=a, m=mm_: e.tensor_tensor(m(1), a(2), a(3), ALU.min))
                    b.op("vector", lambda e, a=a, m=mm_: e.tensor_tensor(m(2), a(0), a(1), ALU.max))
                    fm = b.op("vector", lambda e, a=a, m=mm_: e.tensor_tensor(m(3), a(2), a(3), ALU.max),
                              inc="sdv")
                    b.op("vector", lambda e, m=mm_, z=z: e.tensor_tensor(z(0), m(0), m(1), ALU.max),
                         swait=[("sdv", fm)])
                    fz = b.op("vector", lambda e, m=mm_, z=z: e.tensor_tensor(z(1), m(2), m(3), ALU.min),
                              inc="sdv")
                    fz2 = b.op("vector", lambda e, z=z: e.tensor_tensor(z(2), z(0), z(1), ALU.max),
                               swait=[("sdv", fz)], inc="sdv")
                    km = sp3[1].rearrange("p (n f) -> p n f", f=4)
                    for i in range(4):
                        b.op("vector", lambda e, i=i, a=a, km=km, z=z: e.tensor_tensor(
                            km[:, :, i], a(i), z(2), ALU.is_ge),
                            swait=[("sdv", fz2)] if i == 0 else (),
                            inc="sdv" if i == 3 else None)
                    fk = b.now("sdv")
                    hm_mark[key] = b.op("vector", lambda e, tb=tb: e.tensor_tensor(
                        hsp2[tb], hb2[tb], sp3[1], ALU.mult),
                        swait=[("sdv", fk)], inc="sdv")

                def emit_back(t):
                    tb = t % 2
                    pA = (t % 2) * 2
                    key = (ic, t)
                    pkey = (ic, t - 2) if t >= 2 else ((ic - 1, nt - 2 + t) if ic >= 1 else None)
                    for s_ in range(4):
                        b.op("tensor", lambda e, s=s_, tb=tb, pA=pA: e.matmul(
                            bank(pA, s * 128, 128), hsp2[tb][:, s * 128:(s + 1) * 128],
                            identb, start=True, stop=True),
                            wait=[("sdv", hm_mark[key])] if s_ == 0 else (),
                            inc="spe" if s_ == 3 else None)
                    trm = b.now("spe")
                    trm_mark[key] = trm
                    evT_mark[key] = b.op("scalar", lambda e, tb=tb, pA=pA: e.copy(
                        hspT2[tb], bank(pA)), wait=[("spe", trm)], inc="sac")
                    wdown = [("sac", evT_mark[key]), ("qg", wld0)]
                    akey = (ic, t - 1) if t >= 1 else ((ic - 1, nt - 1) if ic >= 1 else None)
                    if akey is not None and akey in acc_mark:
                        wdown.append(("sdv", acc_mark[akey]))
                    wdown.append(("qg", wld))
                    for s_ in range(4):
                        for o in range(4):
                            wv = wdown if (s_ == 0 and o == 0) else ()
                            b.op("tensor", lambda e, s=s_, o=o, tb=tb: e.matmul(
                                bank(4 + o), hspT2[tb][:, s * 128:(s + 1) * 128],
                                wdb[:, s * 2048 + o * 512:s * 2048 + (o + 1) * 512],
                                start=(s == 0), stop=(s == 3)),
                                wait=wv,
                                inc="spe" if (s_ == 3 and o == 3) else None)
                    dmm_mark[key] = b.now("spe")

                def emit_acc(t):
                    key = (ic, t)
                    r = 0
                    for o in range(4):
                        r = b.op("vector", lambda e, o=o, t=t, ic=ic: (
                            e.tensor_copy(acc[:, t * 2048 + o * 512:t * 2048 + (o + 1) * 512],
                                          bank(4 + o))
                            if ic == 0 else
                            e.tensor_add(acc[:, t * 2048 + o * 512:t * 2048 + (o + 1) * 512],
                                         acc[:, t * 2048 + o * 512:t * 2048 + (o + 1) * 512],
                                         bank(4 + o))),
                            wait=[("spe", dmm_mark[key])] if o == 0 else (),
                            inc="sdv")
                    acc_mark[key] = r

                for t in range(nt):
                    emit_front(t)
                    if t >= 1:
                        emit_back(t - 1)
                        emit_acc(t - 1)
                emit_back(nt - 1)
                emit_acc(nt - 1)
                pe_chunk[ic] = b.now("spe")
            ra = 0
            for t in range(nt):
                tb = t % 2
                xl = b.dma("sync", xint2[tb], d["xres"][g0 + t * 128:g0 + (t + 1) * 128, :],
                           wait=[("sdv", acc_mark[(NIC - 1, min(t, nt - 1))]),
                                 ("spe", b.now("spe")), ("sdv", ra),
                                 ("qs", b.now("qs"))])
                ra = b.op("vector", lambda e, tb=tb, t=t: e.tensor_add(
                    xint2[tb], xint2[tb], acc[:, t * 2048:(t + 1) * 2048]),
                    wait=[("qs", xl)], inc="sdv")
                if to_xres:
                    b.dma("sync", d["xres"][g0 + t * 128:g0 + (t + 1) * 128, :], xint2[tb],
                          wait=[("sdv", ra)])
                else:
                    b.dma("sync", d["out"][g0 - PADQ + t * 128:g0 - PADQ + (t + 1) * 128, :],
                          xint2[tb], wait=[("sdv", ra)])
            r = b.all_now("sdv", "spe", "qs", "qg", "sac")
            phase_reset()
            return r

        def emit_ffn(row0, nrows, to_xres, wait0):
            sizes = [1024, 1024] if nrows == 2048 else [896, 896, 768]
            done = wait0
            g0 = row0
            for gn in sizes:
                done = emit_ffn_group(g0, gn, to_xres, done)
                g0 += gn
            return done

        f1_done = emit_ffn(0, RE, True, P2b_done)
        if stop_after == "P3":
            b.op("sync", lambda e: e.wait_ge(b.sems["qs"], b.now("qs")))
            b.op("sync", lambda e: e.wait_ge(b.sems["qg"], b.now("qg")))
            b.emit()
            return nc, d

        kb_sb = alloc(4 * RE, BF16)
        vb_sb = alloc(20 * 512, BF16)
        qfm = alloc(16 * OWN, BF16)
        P4_FLOOR = cur[0]

        bwk_sb = alloc(16 * 512, BF16)
        bwv_sb = alloc(16 * 512, BF16)
        xint3 = [alloc(E, F32) for _ in range(2)]
        xn3 = [alloc(E, BF16) for _ in range(2)]
        xnt3_sb = [alloc(16 * 512, BF16) for _ in range(2)]
        _wsplit(b, "gpsimd", bwk_sb.rearrange("p (g n) -> p g n", g=16),
                d["bwk"].rearrange("(g p) n -> g p n", p=128), 4, wait=f1_done)
        bw_ready = _wsplit(b, "gpsimd", bwv_sb.rearrange("p (g n) -> p g n", g=16),
                           d["bwv"].rearrange("(g p) n -> g p n", p=128), 4)
        ev3 = []
        for kb5 in range(5):
            xt = xnt3_sb[kb5 % 2]
            for t4 in range(4):
                rt = kb5 * 4 + t4
                tb = rt % 2
                war = [("sdv", ev3[-2])] if len(ev3) >= 2 else list(f1_done)
                ld = b.dma("sync", xint3[tb], d["xres"][rt * 128:(rt + 1) * 128, :],
                           wait=war)
                emit_ln(xint3[tb], xn3[1 - tb], xn3[tb], wait_ld=[("qs", ld)],
                        war_apply=b.all_now("spe"))
                xnw = b.now("sdv")
                b0 = (t4 % 2) * 4
                for c in range(16):
                    bk = b0 + c // 4
                    b.op("tensor", lambda e, c=c, bk=bk, tb=tb: e.matmul(
                        bank(bk, (c % 4) * 128, 128), xn3[tb][:, c * 128:(c + 1) * 128],
                        identb, start=True, stop=True),
                        wait=([("sdv", xnw), ("qg", bw_ready)]
                              + ([("sdv", ev3[-1])] if ev3 else [])) if c == 0 else (),
                        inc="spe" if c == 15 else None)
                ev = b.op("vector", lambda e, t4=t4, b0=b0, xt=xt: e.tensor_copy(
                    xt[:, t4 * 2048:(t4 + 1) * 2048], ps[:, b0 * 512:(b0 + 4) * 512]),
                    wait=b.all_now("spe") + [("qs", b.now("qs"))], inc="sdv")
                ev3.append(ev)
                b.dma("sync", d["xntb"][rt], xt[:, t4 * 2048:(t4 + 1) * 2048],
                      wait=[("sdv", ev)])
            x3 = xt.rearrange("p (t c n) -> p t c n", t=4, c=16)
            kw = b.now("sdv")
            for kc in range(4):
                for ic in range(16):
                    b.op("tensor", lambda e, kc=kc, ic=ic, x3=x3: e.matmul(
                        bank(kc), bwk_sb[:, ic * 512 + kc * 128:ic * 512 + (kc + 1) * 128],
                        x3[:, :, ic, :], start=(ic == 0), stop=(ic == 15)),
                        wait=[("sdv", kw)] if (kc == 0 and ic == 0) else (),
                        inc="spe" if (kc == 3 and ic == 15) else None)
            kmm = b.now("spe")
            for kc in range(4):
                b.op("vector", lambda e, kc=kc, kb5=kb5: e.tensor_copy(
                    kb_sb[:, kc * RE + kb5 * 512:kc * RE + (kb5 + 1) * 512], bank(kc)),
                    wait=[("spe", kmm)] if kc == 0 else (), inc="sdv" if kc == 3 else None)
            for t4 in range(4):
                rt = kb5 * 4 + t4
                pbk = 4 + (t4 % 2)
                for ic in range(16):
                    b.op("tensor", lambda e, t4=t4, ic=ic, pbk=pbk, xt=xt: e.matmul(
                        bank(pbk), xt[:, (t4 * 16 + ic) * 128:(t4 * 16 + ic + 1) * 128],
                        bwv_sb[:, ic * 512:(ic + 1) * 512],
                        start=(ic == 0), stop=(ic == 15)),
                        wait=[("sdv", b.now("sdv"))] if ic == 0 else (),
                        inc="spe" if ic == 15 else None)
                b.op("vector", lambda e, rt=rt, pbk=pbk: e.tensor_copy(
                    vb_sb[:, rt * 512:(rt + 1) * 512], bank(pbk)),
                    wait=b.all_now("spe"), inc="sdv")
        P4a_done = b.all_now("sdv", "spe", "qs", "qg", "sac")
        phase_reset(P4_FLOOR)

        bwq_sb = alloc(16 * E, BF16)
        xntq3 = alloc(4 * E, BF16)
        _wsplit(b, "gpsimd", bwq_sb.rearrange("p (g n) -> p g n", g=16),
                d["bwq"].rearrange("(g p) n -> g p n", p=128), 4, wait=P4a_done)
        bq_ready = b.now("qg")
        for qb4 in range(4):
            lds = 0
            for t4 in range(4):
                rt = 4 + qb4 * 4 + t4
                lds = b.dma("sync", xntq3[:, t4 * E:(t4 + 1) * E], d["xntb"][rt],
                            wait=list(P4a_done) + [("spe", b.now("spe"))])
            xq3b = xntq3.rearrange("p (t c n) -> p t c n", t=4, c=16)
            for half in range(2):
                for oc in range(8):
                    occ = half * 8 + oc
                    for ic in range(16):
                        b.op("tensor", lambda e, occ=occ, oc=oc, ic=ic, xq3b=xq3b: e.matmul(
                            bank(oc), bwq_sb[:, ic * E + occ * 128:ic * E + occ * 128 + 128],
                            xq3b[:, :, ic, :], start=(ic == 0), stop=(ic == 15)),
                            wait=[("qs", lds), ("qg", bq_ready), ("sdv", b.now("sdv"))]
                            if (oc == 0 and ic == 0) else (),
                            inc="spe" if (oc == 7 and ic == 15) else None)
                qmm = b.now("spe")
                for oc in range(8):
                    occ = half * 8 + oc
                    b.op("vector", lambda e, occ=occ, oc=oc, qb4=qb4: e.tensor_copy(
                        qfm[:, occ * OWN + qb4 * 512:occ * OWN + (qb4 + 1) * 512], bank(oc)),
                        wait=[("spe", qmm)] if oc == 0 else (),
                        inc="sdv" if oc == 7 else None)
        P4b_done = b.all_now("sdv", "spe", "qs", "qg")
        phase_reset(P4_FLOOR)

        bwo_sb = alloc(16 * E, BF16)
        bm_sb = [alloc(5 * 128, BF16) for _ in range(2)]
        pb_sb = [alloc(640, BF16) for _ in range(2)]
        abf = alloc(16 * 128, BF16)
        lrow = alloc(16 * 128, F32, rows=1)
        linv = alloc(16 * 128, F32, rows=1)
        xint4_t = alloc(E, F32)
        xint4 = [xint4_t, xint4_t]
        _wsplit(b, "gpsimd", bwo_sb.rearrange("p (g n) -> p g n", g=16),
                d["bwo"].rearrange("(g p) n -> g p n", p=128), 4, wait=P4b_done)
        bo_ready = b.now("qg")
        for qt in range(16):
            mb = qt % 2
            bml = b.dma("sync", bm_sb[mb].rearrange("p (g n) -> p g n", g=5),
                        d["bmask"][qt].rearrange("g p n -> p g n"),
                        wait=b.all_now("sdv") + [("spe", b.now("spe"))])
            xl = b.dma("sync", xint4[mb], d["xres"][(4 + qt) * 128:(5 + qt) * 128, :],
                       wait=b.all_now("sdv") + [("qs", b.now("qs"))])
            abf_dv = 0
            evac_mark = {}
            exp_mark4 = {}
            avm_mark = {}
            for hh in range(HB):
                kvh = hh // 4
                hp = hh % 2
                pb = pb_sb[hp]
                for et in range(5):
                    bk, off = (hp * 2, et * 128) if et < 4 else (hp * 2 + 1, 0)
                    w = []
                    if et == 0:
                        w = [("qs", bml), ("qg", bo_ready)]
                        if hh >= 2:
                            w += [("sac", exp_mark4[hh - 2]), ("sdv", evac_mark[hh - 2])]
                        else:
                            w += [("sac", b.now("sac")), ("sdv", b.now("sdv"))]
                    b.op("tensor", lambda e, kvh=kvh, qt=qt, et=et, bk=bk, off=off, hh=hh: e.matmul(
                        bank(bk, off, 128),
                        kb_sb[:, kvh * RE + (qt + et) * 128:kvh * RE + (qt + et + 1) * 128],
                        qfm[:, hh * OWN + qt * 128:hh * OWN + (qt + 1) * 128],
                        start=True, stop=True), wait=w, inc="spe" if et == 4 else None)
                smm = b.now("spe")
                wexp = [("spe", smm)]
                if hh >= 2:
                    wexp.append(("spe", avm_mark[hh - 2]))
                exp_mark4[hh] = b.op("scalar", lambda e, pb=pb, hp=hp: e.activation(
                    pb, ps[:, hp * 1024:hp * 1024 + 640], AF.Exp, scale=SCALE_B),
                    wait=wexp, inc="sac")
                mk = b.op("vector", lambda e, pb=pb, mb=mb: e.tensor_mul(pb, pb, bm_sb[mb]),
                          wait=[("sac", exp_mark4[hh])], inc="sdv")
                for et in range(5):
                    ww = [("sdv", mk)]
                    if et == 0 and hh >= 2:
                        ww.append(("sdv", evac_mark[hh - 2]))
                    b.op("tensor", lambda e, pb=pb, et=et, hp=hp: e.matmul(
                        bank(4 + hp, 0, 128)[0:1, :], onesb, pb[:, et * 128:(et + 1) * 128],
                        start=(et == 0), stop=(et == 4)),
                        wait=ww if et == 0 else ())
                for et in range(5):
                    b.op("tensor", lambda e, pb=pb, et=et, qt=qt, kvh=kvh, hp=hp: e.matmul(
                        bank(6 + hp, 0, 128), vb_sb[:, (qt + et) * 512 + kvh * 128:(qt + et) * 512 + (kvh + 1) * 128],
                        pb[:, et * 128:(et + 1) * 128], start=(et == 0), stop=(et == 4)),
                        inc="spe" if et == 4 else None)
                avm = b.now("spe")
                avm_mark[hh] = avm
                b.op("vector", lambda e, hh=hh, hp=hp: e.tensor_copy(
                    abf[:, hh * 128:(hh + 1) * 128], bank(6 + hp, 0, 128)), wait=[("spe", avm)])
                abf_dv = b.op("vector", lambda e, hh=hh, hp=hp: e.tensor_copy(
                    lrow[0:1, hh * 128:(hh + 1) * 128], bank(4 + hp, 0, 128)[0:1, :]),
                    inc="sdv")
                evac_mark[hh] = abf_dv
            b.op("vector", lambda e: e.reciprocal(linv, lrow), swait=[("sdv", abf_dv)],
                 inc="sdv")
            rec = b.now("sdv")
            for hh in range(16):
                bk = 4 + hh // 4
                b.op("tensor", lambda e, hh=hh, bk=bk: e.matmul(
                    bank(bk, (hh % 4) * 128, 128), onesrow,
                    linv[0:1, hh * 128:(hh + 1) * 128], start=True, stop=True),
                    wait=[("sdv", rec)] if hh == 0 else (),
                    inc="spe" if hh == 15 else None)
            bcm = b.now("spe")
            sc = 0
            for hh in range(16):
                bk = 4 + hh // 4
                sc = b.op("vector", lambda e, hh=hh, bk=bk: e.tensor_tensor(
                    abf[:, hh * 128:(hh + 1) * 128], abf[:, hh * 128:(hh + 1) * 128],
                    bank(bk, (hh % 4) * 128, 128), ALU.mult),
                    wait=[("spe", bcm)] if hh == 0 else (), inc="sdv")
            for do in range(4):
                for hh in range(16):
                    b.op("tensor", lambda e, do=do, hh=hh: e.matmul(
                        bank(4 + do), abf[:, hh * 128:(hh + 1) * 128],
                        bwo_sb[:, hh * E + do * 512:hh * E + (do + 1) * 512],
                        start=(hh == 0), stop=(hh == 15)),
                        wait=[("sdv", sc)] if (do == 0 and hh == 0) else (),
                        inc="spe" if (do == 3 and hh == 15) else None)
            wmm = b.now("spe")
            ra = b.op("vector", lambda e, mb=mb: e.tensor_add(
                xint4[mb], xint4[mb], ps[:, 4 * 512:8 * 512]),
                wait=[("spe", wmm), ("qs", xl)], inc="sdv")
            b.dma("sync", d["xres"][(4 + qt) * 128:(5 + qt) * 128, :], xint4[mb],
                  wait=[("sdv", ra)])
        P4_done = b.all_now("sdv", "spe", "qs", "qg", "sac")
        phase_reset()

        if stop_after == "P4":
            b.op("sync", lambda e: e.wait_ge(b.sems["qs"], b.now("qs")))
            b.op("sync", lambda e: e.wait_ge(b.sems["qg"], b.now("qg")))
            b.emit()
            return nc, d

        emit_ffn(PADQ, OWN, False, P4_done)

        b.op("sync", lambda e: e.wait_ge(b.sems["qs"], b.now("qs")))
        b.op("sync", lambda e: e.wait_ge(b.sems["qg"], b.now("qg")))
        b.emit()
    return nc, d


def kernel(**inputs):
    maps = stage_inputs(inputs)
    nc, _ = build_program()
    res = run_bass_kernel_spmd(nc, maps, core_ids=list(range(NCORE)))
    out = np.empty((B, T, E), dtype=np.float32)
    for c in range(NCORE):
        bb, h = c // 2, c % 2
        out[bb, h * 2048:(h + 1) * 2048, :] = res.results[c]["out"]
    return out
